# version 05
# baseline (speedup 1.0000x reference)
import os
import numpy as np
import ml_dtypes
from contextlib import ExitStack

_TRUNC = int(os.environ.get("KTRUNC", "99"))
_KSUB = int(os.environ.get("KSUB", "99"))

import concourse.bass as bass
import concourse.mybir as mybir
from concourse.bass_utils import run_bass_kernel_spmd

F32 = mybir.dt.float32
BF16 = mybir.dt.bfloat16
ALU = mybir.AluOpType
AF = mybir.ActivationFunctionType
AX = mybir.AxisListType

D = 2048
TPC = 1024
H = 64
LC = 256
KCV = 4
INTER = 4096
CONV = 4352
NCT = 34
NK = 16
NFT = 32
EPS = 1e-5

ENGS = ("sync", "gpsimd", "vector", "scalar", "tensor")
_PEND = object()


class Prog:

    def __init__(self, nc, es):
        self.nc = nc
        self.items = {e: [] for e in ENGS}
        self.cnt = {}
        self.sems = {}
        self.waited = {e: {} for e in ENGS}
        self.es = es
        self.last_w = {}
        self.last_r = {}
        self.esem = {"sync": "ld", "gpsimd": "gp", "vector": "dv",
                     "scalar": "ac", "tensor": "pe"}
        self.pending = {e: [] for e in ENGS}

    def sem(self, name):
        if name not in self.sems:
            self.sems[name] = self.es.enter_context(self.nc.semaphore(name))
            self.cnt[name] = 0
        return self.sems[name]

    def _wait(self, eng, tok):
        if tok is None:
            return
        if isinstance(tok, tuple) and tok[0] is _PEND:
            if tok[1] == eng:
                return
            raise AssertionError(
                f"{eng} depends on un-inc'd op of {tok[1]} -- add inc=True there")
        s, v = tok
        if self.waited[eng].get(s, 0) >= v:
            return
        self.waited[eng][s] = v
        self.sem(s)
        self.items[eng].append(("wait", s, v))

    def op(self, eng, fn, reads=(), writes=(), inc=True, n=1):
        for k in reads:
            self._wait(eng, self.last_w.get(k))
        for k in writes:
            self._wait(eng, self.last_w.get(k))
            for t in self.last_r.get(k, ()):
                self._wait(eng, t)
        tok = None
        if inc:
            s = self.esem[eng]
            self.sem(s)
            self.cnt[s] += n
            tok = (s, self.cnt[s])
            for ent in self.pending[eng]:
                kind, key, idx = ent
                if kind == "w":
                    if self.last_w.get(key) is _pendmark(eng):
                        self.last_w[key] = tok
                else:
                    lr = self.last_r.get(key, [])
                    for i2, t2 in enumerate(lr):
                        if t2 is _pendmark(eng):
                            lr[i2] = tok
            self.pending[eng] = []
        self.items[eng].append(("op", fn, self.esem[eng] if inc else None, n))
        mark = tok if inc else _pendmark(eng)
        for k in writes:
            self.last_w[k] = mark
            self.last_r[k] = []
        for k in reads:
            self.last_r.setdefault(k, []).append(mark)
        if not inc:
            for k in writes:
                self.pending[eng].append(("w", k, None))
            for k in reads:
                self.pending[eng].append(("r", k, None))
        return tok

    def fence(self):
        for eng in ENGS:
            for s, c in list(self.cnt.items()):
                if c > 0:
                    self._wait(eng, (s, c))

    def replay(self):
        with self.nc.Block() as block:
            for eng in ENGS:
                def runner(e, eng=eng):
                    for item in self.items[eng]:
                        if item[0] == "op":
                            _, fn, incname, n = item
                            inst = fn(e)
                            if incname is not None:
                                inst.then_inc(self.sems[incname], n)
                        else:
                            _, s, v = item
                            e.wait_ge(self.sems[s], v)
                getattr(block, eng)(runner)


_pendmarks = {}
def _pendmark(eng):
    if eng not in _pendmarks:
        _pendmarks[eng] = (_PEND, eng)
    return _pendmarks[eng]


class Bump:
    def __init__(self, t, cap):
        self.t, self.cap, self.top, self.marks = t, cap, 0, []
    def push(self):
        self.marks.append(self.top)
    def pop(self):
        self.top = self.marks.pop()
    def f32(self, n, parts=128):
        o = self.top
        assert o + n <= self.cap, f"scratch overflow {o}+{n}>{self.cap}"
        self.top += n
        return self.t.ap()[0:parts, o:o + n]
    def bf16(self, n, parts=128):
        assert n % 2 == 0
        o = self.top
        assert o + n // 2 <= self.cap, f"scratch overflow {o}+{n // 2}>{self.cap}"
        self.top += n // 2
        return self.t.ap().bitcast(BF16)[0:parts, 2 * o:2 * o + n]


def build_nc():
    nc = bass.Bass("TRN2", target_bir_lowering=False)
    es = ExitStack()
    pr = Prog(nc, es)

    def din(name, shape):
        return nc.dram_tensor(name, list(shape), F32, kind="ExternalInput").ap()

    x_main = din("x_main", (TPC, D))
    x_halo = din("x_halo", (TPC, D))
    wa_t = din("wa_t", (NCT, 128, NK * 128))
    wa_dt = din("wa_dt", (128, NK * 64))
    wb_t = din("wb_t", (NFT, 128, NK * 128))
    wo_t = din("wo_t", (NFT, 128, NK * 128))
    w1_t = din("w1_t", (8, NFT, 128, NK * 128))
    w3_t = din("w3_t", (8, NFT, 128, NK * 128))
    w2_n = din("w2_n", (8, INTER, D))
    wg_t = din("wg_t", (128, NK * 8))
    conv_w_t = din("conv_w_t", (128, NCT * KCV))
    ba_conv = din("ba_conv", (128, NCT))
    ba_dtc = din("ba_dtc", (64, 1))
    bg_t = din("bg_t", (128, NFT))
    bo_t = din("bo_t", (128, 16))
    dt_bias = din("dt_bias", (64, 1))
    a_log = din("a_log", (64, 1))
    d_exp = din("d_exp", (128, 32))
    nw_t = din("nw_t", (128, 32))
    hmask = din("hmask", (128, 1))
    id128f = din("id128f", (128, 128))
    id64f = din("id64f", (64, 64))
    id64d_in = din("id64d_in", (128, 64))
    maskc = din("maskc", (128, 2 * LC))
    diagneg = din("diagneg", (128, 128))
    lstrict = din("lstrict", (128, 2 * LC))
    o_out = nc.dram_tensor("o_out", [TPC, D], F32, kind="ExternalOutput").ap()

    def sbt(name, shape, dtype):
        return es.enter_context(nc.sbuf_tensor(name, list(shape), dtype))

    xn_t = sbt("xn_t", (128, NK * TPC), BF16)
    xn = [xn_t.ap()[:, k * TPC:(k + 1) * TPC] for k in range(NK)]
    arena_t = sbt("arena_t", (128, 16384), F32)
    arena = arena_t.ap()
    arena_bf = arena_t.ap().bitcast(BF16)

    def hs_slot(ft, c):
        return arena_bf[:, (ft // 2) * 2048 + c * 512 + (ft % 2) * 256:][:, 0:256]

    def acc_slot(dt_i, c):
        return arena[:, dt_i * 1024 + c * 256:][:, 0:256]

    def acc_row(dt_i):
        return arena[:, dt_i * 1024:(dt_i + 1) * 1024]

    prev3_t = sbt("prev3_t", (128, (NCT + 1) * 3), F32)
    id128b = sbt("id128b", (128, 128), BF16)
    id128fs = sbt("id128fs", (128, 128), F32)
    id64b = sbt("id64b", (64, 64), BF16)
    id64d = sbt("id64d", (128, 64), BF16)
    diagn_b = sbt("diagn_b", (128, 128), BF16)
    lstr_b = sbt("lstr_b", (128, 2 * LC), BF16)
    id64fs = sbt("id64fs", (64, 64), F32)
    ones1b = sbt("ones1b", (1, 128), BF16)
    onescb = sbt("onescb", (128, 1), BF16)
    mks_t = sbt("mks_t", (128, 2 * LC), F32)
    convw = sbt("convw", (128, NCT * KCV), F32)
    sb_ba = sbt("sb_ba", (128, NCT), F32)
    sb_badt = sbt("sb_badt", (64, 1), F32)
    sb_bg = sbt("sb_bg", (128, NFT), F32)
    sb_bo = sbt("sb_bo", (128, 16), F32)
    sb_dtb = sbt("sb_dtb", (64, 1), F32)
    sb_alog = sbt("sb_alog", (64, 1), F32)
    aneg = sbt("aneg", (64, 1), F32)
    badt_comb = sbt("badt_comb", (64, 1), F32)
    sb_dexp = sbt("sb_dexp", (128, 32), F32)
    sb_nw = sbt("sb_nw", (128, 32), F32)
    sb_hm = sbt("sb_hm", (128, 1), F32)
    sb_wg = sbt("sb_wg", (128, NK * 8), BF16)
    stat = sbt("stat", (128, 8), F32)
    wfm = sbt("wfm", (8, TPC), BF16)
    r8s = sbt("r8s", (128, 64), F32)
    onesr = sbt("onesr", (64, LC), F32)
    epsc = sbt("epsc", (128, 1), F32)

    SCCAP = 26032
    sc_t = sbt("sc_t", (128, SCCAP), F32)
    SC = Bump(sc_t, SCCAP)

    psb = [es.enter_context(nc.psum_tensor(f"psb{i}", [128, 512], F32)) for i in range(8)]
    def psf(i, n=512, p=128):
        return psb[i].ap()[0:p, 0:n]
    def psh(i, n=1024, p=128):
        return psb[i].ap().bitcast(BF16)[0:p, 0:n]

    NDMASEM = 16
    dma_ctr = {"sync": 0, "gpsimd": 0}

    def dma(q, dst, src, reads=(), writes=(), inc=True):
        i = dma_ctr[q]
        dma_ctr[q] = i + 1
        semname = f"dq_{q}_{i % NDMASEM}"
        pr.sem(semname)
        if i >= NDMASEM:
            pr._wait(q, (semname, 16 * (i // NDMASEM)))
        for k in reads:
            pr._wait(q, pr.last_w.get(k))
        for k in writes:
            pr._wait(q, pr.last_w.get(k))
            for t in pr.last_r.get(k, ()):
                pr._wait(q, t)
        pr.cnt[semname] += 16
        tok = (semname, pr.cnt[semname])
        pr.items[q].append(("op", lambda e, d=dst, s=src: e.dma_start(out=d, in_=s),
                            semname, 16))
        for k in writes:
            pr.last_w[k] = tok
            pr.last_r[k] = []
        for k in reads:
            pr.last_r.setdefault(k, []).append(tok)
        return tok

    def mm(out, lhsT, rhs, start, stop, reads=(), writes=(), inc=True):
        return pr.op("tensor", lambda e, o=out, l=lhsT, r=rhs, a=start, b=stop:
                     e.matmul(o, l, r, start=a, stop=b),
                     reads=reads, writes=writes, inc=inc)

    def tr(out, in_, ident, reads=(), writes=(), inc=True):
        return pr.op("tensor", lambda e, o=out, i=in_, d=ident: e.transpose(o, i, d),
                     reads=reads, writes=writes, inc=inc)

    def tt(out, a, b, opk, reads=(), writes=(), eng="vector"):
        return pr.op(eng, lambda e, o=out, x=a, y=b, k=opk: e.tensor_tensor(o, x, y, op=k),
                     reads=reads, writes=writes)

    def ts(out, a, s1, s2, op0, op1=None, reads=(), writes=(), accum=None, eng="vector"):
        if op1 is None:
            return pr.op(eng, lambda e, o=out, x=a, s=s1: e.tensor_scalar(
                o, x, s, None, op0=op0), reads=reads, writes=writes)
        return pr.op(eng, lambda e, o=out, x=a, s=s1, s2=s2: e.tensor_scalar(
            o, x, s, s2, op0=op0, op1=op1), reads=reads, writes=writes)

    def stt(out, a, s1, b, op0, op1, reads=(), writes=()):
        return pr.op("vector", lambda e, o=out, x=a, s=s1, y=b: e.scalar_tensor_tensor(
            o, x, s, y, op0=op0, op1=op1), reads=reads, writes=writes)

    def act(out, in_, func, bias=0.0, scale=1.0, accum=None, reads=(), writes=()):
        return pr.op("scalar", lambda e, o=out, i=in_, f=func, b=bias, s=scale, a=accum:
                     e.activation(o, i, f, bias=b, scale=s, accum_out=a),
                     reads=reads, writes=writes)

    def cp(out, in_, reads=(), writes=(), eng="vector"):
        return pr.op(eng, lambda e, o=out, i=in_: e.tensor_copy(o, i),
                     reads=reads, writes=writes)

    def mset(ap_, v, reads=(), writes=()):
        return pr.op("vector", lambda e, o=ap_, v=v: e.memset(o, v),
                     reads=reads, writes=writes)

    for dst, src, key in [
        (convw.ap(), conv_w_t, "convw"), (sb_ba.ap(), ba_conv, "sb_ba"),
        (sb_badt.ap(), ba_dtc, "sb_badt"), (sb_bg.ap(), bg_t, "sb_bg"),
        (sb_bo.ap(), bo_t, "sb_bo"), (sb_dtb.ap(), dt_bias, "sb_dtb"),
        (sb_alog.ap(), a_log, "sb_alog"), (sb_dexp.ap(), d_exp, "sb_dexp"),
        (sb_nw.ap(), nw_t, "sb_nw"), (sb_hm.ap(), hmask, "sb_hm"),
        (id128fs.ap(), id128f, "id128fs"), (id64fs.ap(), id64f, "id64fs"),
        (mks_t.ap(), maskc, "mks"),
    ]:
        dma("sync", dst, src, writes=(key,))
    for dst, src, key in [(id128b.ap(), id128f, "id128b"), (id64b.ap(), id64f, "id64b"),
                          (id64d.ap(), id64d_in, "id64d"), (sb_wg.ap(), wg_t, "sb_wg"),
                          (diagn_b.ap(), diagneg, "diagn"), (lstr_b.ap(), lstrict, "lstr")]:
        dma("gpsimd", dst, src, writes=(key,))
    mset(ones1b.ap(), 1.0, writes=("ones1b",))
    mset(onescb.ap(), 1.0, writes=("onescb",))
    mset(onesr.ap(), 1.0, writes=("onesr",))
    mset(prev3_t.ap(), 0.0, writes=("prev3",))
    mset(epsc.ap(), EPS, writes=("epsc",))
    act(aneg.ap(), sb_alog.ap(), AF.Exp, reads=("sb_alog",), writes=("aneg",))
    tt(badt_comb.ap(), sb_badt.ap(), sb_dtb.ap(), ALU.add,
       reads=("sb_badt", "sb_dtb"), writes=("badtc",))
    ts(aneg.ap(), aneg.ap(), -1.0, None, ALU.mult, reads=(), writes=("aneg",))

    SC.push()
    regA = SC.f32(6700)
    ln_a = regA[:, 0:D]
    ln_b = regA[:, D:2 * D]
    ln_q = regA[:, 2 * D:3 * D]
    yreg = SC.f32(8192)
    xnh = [yreg.bitcast(BF16)[:, k * TPC:(k + 1) * TPC] for k in range(NK)]

    def emit_ln(x_hbm, xout, xkey):
        for i in range(8):
            buf = ln_a if i % 2 == 0 else ln_b
            lq = ln_q[:, (i % 2) * 1024:(i % 2) * 1024 + D] if False else ln_q
            bk = f"lnbuf{i % 2}"
            so = (i % 2) * 4
            dma("sync", buf, x_hbm[i * 128:(i + 1) * 128, :], writes=(bk,))
            pr.op("vector", lambda e, b=buf, so=so: e.reduce_sum(
                stat.ap()[:, so:so + 1], b, axis=AX.X, negate=True),
                reads=(bk,), writes=(f"st0{so}",))
            ts(stat.ap()[:, so:so + 1], stat.ap()[:, so:so + 1], 1.0 / D, None,
               ALU.mult, reads=(), writes=(f"st0{so}",))
            ts(ln_q, buf, stat.ap()[:, so:so + 1], None, ALU.add,
               reads=(bk, f"st0{so}"), writes=("lnq",))
            act(buf, ln_q, AF.Square, accum=stat.ap()[:, so + 1:so + 2],
                reads=("lnq",), writes=(bk, f"st1{so}"))
            act(stat.ap()[:, so + 2:so + 3], stat.ap()[:, so + 1:so + 2], AF.Sqrt,
                bias=epsc.ap(), scale=1.0 / D,
                reads=(f"st1{so}", "epsc"), writes=(f"st2{so}",))
            pr.op("vector", lambda e, so=so: e.reciprocal(
                stat.ap()[:, so + 3:so + 4], stat.ap()[:, so + 2:so + 3]),
                reads=(f"st2{so}",), writes=(f"st3{so}",))
            ts(buf.bitcast(BF16)[:, 0:D], ln_q, stat.ap()[:, so + 3:so + 4], None,
               ALU.mult, reads=("lnq", f"st3{so}"), writes=(bk,))
            for g4 in range(4):
                bank = g4 % 2
                tks = None
                for j in range(4):
                    k = g4 * 4 + j
                    tks = tr(psh(bank)[:, j * 128:(j + 1) * 128],
                             buf.bitcast(BF16)[:, k * 128:(k + 1) * 128], id128b.ap(),
                             reads=(bk, "id128b"), writes=(f"ps{bank}",),
                             inc=(j == 3))
                for j in range(4):
                    k = g4 * 4 + j
                    cp(xout[k][:, i * 128:(i + 1) * 128],
                       psh(bank)[:, j * 128:(j + 1) * 128],
                       reads=(f"ps{bank}",), writes=(f"{xkey}{k}",))

    emit_ln(x_main, xn, "xn")
    if _TRUNC >= 2:
        emit_ln(x_halo, xnh, "xnh")
    XN = tuple(f"xn{k}" for k in range(NK))
    XNH = tuple(f"xnh{k}" for k in range(NK))

    for i in range(8 if _TRUNC >= 3 else 0):
        bank = 2 + (i % 2)
        for k in range(NK):
            mm(psf(bank, 8), xn[k][:, i * 128:(i + 1) * 128],
               sb_wg.ap()[:, k * 8:(k + 1) * 8], k == 0, k == NK - 1,
               reads=(f"xn{k}", "sb_wg"), writes=(f"ps{bank}",), inc=(k == NK - 1))
        act(r8s.ap()[:, 0:8], psf(bank, 8), AF.Tanh, scale=1.0 / 30.0,
            reads=(f"ps{bank}",), writes=("r8a",))
        pr.op("vector", lambda e: e.reduce_max(
            r8s.ap()[:, 8:9], r8s.ap()[:, 0:8], axis=AX.X, negate=True),
            reads=("r8a",), writes=("r8m",))
        act(r8s.ap()[:, 16:24], r8s.ap()[:, 0:8], AF.Exp, bias=r8s.ap()[:, 8:9],
            scale=30.0, accum=r8s.ap()[:, 9:10], reads=("r8a", "r8m"),
            writes=("r8e", "r8sum"))
        pr.op("vector", lambda e: e.reciprocal(r8s.ap()[:, 10:11], r8s.ap()[:, 9:10]),
              reads=("r8sum",), writes=("r8rc",))
        ts(r8s.ap()[:, 24:32], r8s.ap()[:, 16:24], r8s.ap()[:, 10:11], None, ALU.mult,
           reads=("r8e", "r8rc"), writes=("r8p",))
        pr.op("vector", lambda e: e.max(out=r8s.ap()[:, 32:40], in_=r8s.ap()[:, 24:32]),
              reads=("r8p",), writes=("r8t",))
        ts(r8s.ap()[:, 40:48], r8s.ap()[:, 24:32], r8s.ap()[:, 33:34], None, ALU.is_ge,
           reads=("r8p", "r8t"), writes=("r8k",))
        tt(r8s.ap()[:, 48:56], r8s.ap()[:, 24:32], r8s.ap()[:, 40:48], ALU.mult,
           reads=("r8p", "r8k"), writes=("r8w",))
        tr(psb[bank].ap()[0:8, 0:128], r8s.ap()[:, 48:56], id128fs.ap(),
           reads=("r8w", "id128fs"), writes=(f"ps{bank}",))
        cp(wfm.ap()[0:8, i * 128:(i + 1) * 128], psb[bank].ap()[0:8, 0:128],
           reads=(f"ps{bank}",), writes=("wfm",))

    pr.fence()
    SC.push()
    proj_c = [regA[:, i * 520:i * 520 + 516] for i in range(2)]
    cacc_b = [regA[:, 1040 + i * 512:1040 + (i + 1) * 512] for i in range(2)]
    wab = [regA.bitcast(BF16)[:, 2 * (2064 + i * 2048):][:, 0:4096] for i in range(2)]
    wdtb = regA.bitcast(BF16)[:, 2 * 6160:][:, 0:1024]
    S = SC.f32(INTER)
    bmcm_ap = SC.bf16(1024)
    dtb_ap = SC.f32(512, parts=64)
    acum_ = SC.f32(LC, parts=64)
    ea16_ = SC.bf16(LC, parts=64)
    wsc_ = SC.f32(LC, parts=64)
    dtc_ = SC.f32(LC, parts=64)
    abt_ = SC.f32(LC, parts=64)
    negacumT_ = SC.f32(128)
    dtT_ = SC.f32(128)
    wscT_ = SC.bf16(128)
    bmTM_ = SC.bf16(256)
    exal_ = SC.bf16(2, parts=64)
    rowd_ = SC.bf16(64, parts=1)
    gt_ = SC.f32(2 * LC)
    s16g_ = SC.bf16(512)
    hsTM_ = SC.bf16(1024)
    hsW_ = SC.bf16(1024)
    _prh_off = SC.top
    prh = [SC.f32(LC) for _ in range(2)]
    erh = [SC.f32(LC) for _ in range(2)]
    mth = [SC.bf16(LC) for _ in range(4)]
    ea_s = [SC.bf16(LC) for _ in range(2)]
    cmsc = [SC.bf16(LC) for _ in range(2)]
    sgroll = [mth[2], mth[3]]
    sqroll = [sc_t.ap().bitcast(BF16)[:, 2 * (_prh_off + i * LC):][:, 0:LC]
              for i in range(2)]
    rst_row_ap = SC.f32(LC, parts=1)
    rst16_ap = SC.bf16(LC, parts=1)

    class _A:
        def __init__(self, ap_):
            self._ap = ap_
        def ap(self):
            return self._ap
    acum, ea16, wsc, dtc, abt = _A(acum_), _A(ea16_), _A(wsc_), _A(dtc_), _A(abt_)
    negacumT, dtT, wscT, bmTM = _A(negacumT_), _A(dtT_), _A(wscT_), _A(bmTM_)
    exal, rowd, gt_sb, s16g = _A(exal_[:, 0:1]), _A(rowd_), _A(gt_), _A(s16g_)
    rst_row, rst16 = _A(rst_row_ap), _A(rst16_ap)
    hsTM_t, hsW_t = _A(hsTM_), _A(hsW_)
    bmcm_t, dtb_t = _A(bmcm_ap), _A(dtb_ap)

    def y_tile(ft):
        return yreg[:, ft * 256:(ft + 1) * 256]

    def sg_tile(i):
        return sgroll[i % 2]

    def scanprep(region, pair, xsrc, XK):
        tok0 = pair * 512
        for cti in range(NCT + 1):
            bi = cti % 2
            if cti % 2 == 0 and cti < NCT:
                wbuf = (cti // 2) % 2
                for cti2 in range(cti, min(cti + 2, NCT)):
                    dma("gpsimd",
                        wab[wbuf][:, (cti2 % 2) * NK * 128:][:, 0:NK * 128],
                        wa_t[cti2, :, :], writes=(f"wab{wbuf}",))
            if cti == NCT:
                dma("gpsimd", wdtb, wa_dt, writes=("wdtb",))
            for k in range(NK):
                if cti < NCT:
                    wbuf = (cti // 2) % 2
                    lhsTk = wab[wbuf][:, (cti % 2) * NK * 128 + k * 128:][:, 0:128]
                    out_ap = psf(bi, 512)
                    rkeys = (f"wab{wbuf}", XK[k])
                else:
                    lhsTk = wdtb[:, k * 64:(k + 1) * 64]
                    out_ap = psf(bi, 512, p=64)
                    rkeys = ("wdtb", XK[k])
                mm(out_ap, lhsTk, xsrc[k][:, tok0:tok0 + 512], k == 0, k == NK - 1,
                   reads=rkeys, writes=(f"ps{bi}",), inc=(k == NK - 1))
            pk = f"projc{bi}"
            if cti < NCT:
                act(proj_c[bi][:, 3:515], psf(bi, 512), AF.Identity,
                    bias=sb_ba.ap()[:, cti:cti + 1],
                    reads=(f"ps{bi}", "sb_ba"), writes=(pk,))
                cp(proj_c[bi][:, 0:3], prev3_t.ap()[:, cti * 3:cti * 3 + 3],
                   reads=("prev3",), writes=(pk,), eng="gpsimd")
                cp(prev3_t.ap()[:, cti * 3:cti * 3 + 3], proj_c[bi][:, 512:515],
                   reads=(pk,), writes=("prev3",), eng="gpsimd")
                ts(cacc_b[bi], proj_c[bi][:, 0:512], convw.ap()[:, cti * 4:cti * 4 + 1],
                   None, ALU.mult, reads=(pk, "convw"), writes=(f"cacc{bi}",))
                for kk in range(1, KCV):
                    stt(cacc_b[bi], proj_c[bi][:, kk:kk + 512],
                        convw.ap()[:, cti * 4 + kk:cti * 4 + kk + 1], cacc_b[bi],
                        ALU.mult, ALU.add, reads=(pk, "convw"), writes=(f"cacc{bi}",))
                for cc2 in range(2):
                    c = pair * 2 + cc2
                    if cti < 32:
                        dst = hs_slot(cti, c)
                        wk = f"hs{c}_{cti}"
                    else:
                        off = (0 if cti == 32 else 512) + cc2 * 256
                        dst = bmcm_t.ap()[:, off:off + 256]
                        wk = "bmcm"
                    act(dst, cacc_b[bi][:, cc2 * 256:(cc2 + 1) * 256], AF.Silu,
                        reads=(f"cacc{bi}",), writes=(wk,))
            else:
                act(proj_c[bi][0:64, 3:515], psf(bi, 512, p=64), AF.Identity,
                    bias=badt_comb.ap(),
                    reads=(f"ps{bi}", "badtc"), writes=(pk,))
                act(cacc_b[bi][0:64, 0:512], proj_c[bi][0:64, 3:515], AF.Abs,
                    reads=(pk,), writes=(f"cacc{bi}",))
                act(cacc_b[bi][0:64, 0:512], cacc_b[bi][0:64, 0:512], AF.Exp, scale=-1.0,
                    reads=(), writes=(f"cacc{bi}",))
                act(cacc_b[bi][0:64, 0:512], cacc_b[bi][0:64, 0:512], AF.Ln, bias=1.0,
                    reads=(), writes=(f"cacc{bi}",))
                ts(proj_c[bi][0:64, 3:515], proj_c[bi][0:64, 3:515], 0.0, None, ALU.max,
                   reads=(), writes=(pk,))
                tt(dtb_t.ap(), proj_c[bi][0:64, 3:515], cacc_b[bi][0:64, 0:512], ALU.add,
                   reads=(pk, f"cacc{bi}"), writes=("dtb",))

    def scancore(c, cc, out):
        if _KSUB < 1:
            return
        ts(abt.ap(), dtb_t.ap()[:, cc * 256:(cc + 1) * 256], aneg.ap(), None, ALU.mult,
           reads=("dtb", "aneg"), writes=("abt",))
        cp(dtc.ap(), dtb_t.ap()[:, cc * 256:(cc + 1) * 256],
           reads=("dtb",), writes=("dtc",))
        pr.op("vector", lambda e: e.tensor_tensor_scan(
            acum.ap(), onesr.ap(), abt.ap(), 0.0, op0=ALU.mult, op1=ALU.add),
            reads=("onesr", "abt"), writes=("acum",))
        act(ea16.ap(), acum.ap(), AF.Exp, reads=("acum",), writes=("ea16",))
        act(exal.ap(), acum.ap()[:, 255:256], AF.Exp, reads=("acum",), writes=("exal",))
        act(wsc.ap(), acum.ap(), AF.Exp, bias=acum.ap()[:, 255:256], scale=-1.0,
            reads=("acum",), writes=("wsc",))
        tt(wsc.ap(), wsc.ap(), dtc.ap(), ALU.mult, reads=("dtc",), writes=("wsc",))
        if _KSUB < 2:
            return
        for sb_i in range(2):
            for si, src in enumerate((acum, dtc, wsc)):
                tr(psf(1)[:, (si * 2 + sb_i) * 64:][:, 0:64],
                   src.ap()[:, sb_i * 128:(sb_i + 1) * 128], id64fs.ap(),
                   reads=(("acum", "dtc", "wsc")[si], "id64fs"), writes=("ps1",),
                   inc=(sb_i == 1 and si == 2))
        if _KSUB < 3:
            return
        ts(negacumT.ap(), psf(1)[:, 0:128], -1.0, None, ALU.mult,
           reads=("ps1",), writes=("negacumT",))
        cp(dtT.ap(), psf(1)[:, 128:256], reads=("ps1",), writes=("dtT",))
        cp(wscT.ap(), psf(1)[:, 256:384], reads=("ps1",), writes=("wscT",))
        for sb_i in range(2):
            tr(psh(1)[:, sb_i * 128:(sb_i + 1) * 128],
               bmcm_t.ap()[:, cc * 256 + sb_i * 128:][:, 0:128], id128b.ap(),
               reads=("bmcm", "id128b"), writes=("ps1",), inc=False)
        mm(psb[1].ap()[0:1, 256:320], exal.ap(), id64b.ap(), True, True,
           reads=("exal", "id64b"), writes=("ps1",))
        if _KSUB < 4:
            return
        cp(bmTM.ap(), psh(1)[:, 0:256], reads=("ps1",), writes=("bmTM",))
        cp(rowd.ap(), psb[1].ap()[0:1, 256:320], reads=("ps1",), writes=("rowd",))
        if out:
            for sb_i in range(2):
                mm(psf(1)[:, sb_i * 256:(sb_i + 1) * 256],
                   bmcm_t.ap()[:, cc * 256 + sb_i * 128:][:, 0:128],
                   bmcm_t.ap()[:, 512 + cc * 256:][:, 0:256], sb_i == 0, sb_i == 1,
                   reads=("bmcm",), writes=("ps1",), inc=(sb_i == 1))
            cp(gt_sb.ap(), psf(1), reads=("ps1",), writes=("gt",))

        if _KSUB < 5:
            return
        for g in range(8):
            for sb_i in range(2):
                for j in range(4):
                    ft = 4 * g + j
                    tr(psh(2 + sb_i)[:, j * 128:(j + 1) * 128],
                       hs_slot(ft, c)[:, sb_i * 128:(sb_i + 1) * 128],
                       id128b.ap(),
                       reads=(f"hs{c}_{ft}", "id128b"), writes=(f"ps{2 + sb_i}",),
                       inc=(j == 3))
            if _KSUB < 6:
                continue
            cp(hsTM_t.ap()[:, 0:512], psh(2)[:, 0:512], reads=("ps2",), writes=("hsTM",))
            cp(hsTM_t.ap()[:, 512:1024], psh(3)[:, 0:512], reads=("ps3",), writes=("hsTM",))
            for sb_i in range(2):
                for h8 in range(8):
                    h = g * 8 + h8
                    tt(hsW_t.ap()[:, sb_i * 512 + h8 * 64:][:, 0:64],
                       hsTM_t.ap()[:, sb_i * 512 + h8 * 64:][:, 0:64],
                       wscT.ap()[:, sb_i * 64 + h:][:, 0:1].to_broadcast([128, 64]),
                       ALU.mult, reads=("hsTM", "wscT"), writes=("hsW",),
                       eng="gpsimd")
            if _KSUB < 7:
                continue
            cp(s16g.ap(), S[:, g * 512:(g + 1) * 512], reads=("S",), writes=("s16g",))

            if out:
                for h8 in range(8):
                    h = g * 8 + h8
                    pj = h // 2
                    ybank = 6 + (pj % 2)
                    bb = 2 + (h % 2)
                    eb = 4 + (h % 2)
                    mtb = [(h % 2) * 2, (h % 2) * 2 + 1]
                    hp = h % 2
                    for sb_i in range(2):
                        mm(psf(bb)[:, sb_i * 256:(sb_i + 1) * 256],
                           id64fs.ap()[:, h:h + 1].to_broadcast([64, 128]),
                           acum.ap(), sb_i == 0, False, reads=("id64fs", "acum"),
                           writes=(f"ps{bb}",), inc=False)
                        mm(psf(bb)[:, sb_i * 256:(sb_i + 1) * 256],
                           diagn_b.ap(),
                           lstr_b.ap()[:, sb_i * 256:(sb_i + 1) * 256],
                           False, sb_i == 1, reads=("diagn", "lstr"),
                           writes=(f"ps{bb}",), inc=(sb_i == 1))
                    mm(psf(eb)[:, 0:256], id64b.ap()[:, h:h + 1].to_broadcast([64, 128]),
                       ea16.ap(), True, True, reads=("id64b", "ea16"),
                       writes=(f"ps{eb}",))
                    for sb_i in range(2):
                        act(erh[sb_i], psf(bb)[:, sb_i * 256:(sb_i + 1) * 256], AF.Exp,
                            bias=negacumT.ap()[:, sb_i * 64 + h:][:, 0:1],
                            reads=(f"ps{bb}", "negacumT"), writes=(f"erh{sb_i}",))
                        stt(mth[mtb[sb_i]], erh[sb_i],
                            dtT.ap()[:, sb_i * 64 + h:][:, 0:1],
                            gt_sb.ap()[:, sb_i * 256:(sb_i + 1) * 256],
                            ALU.mult, ALU.mult,
                            reads=(f"erh{sb_i}", "dtT", "gt"),
                            writes=(f"mth{mtb[sb_i]}",))
                    act(ea_s[hp], psf(eb)[:, 0:256], AF.Identity,
                        reads=(f"ps{eb}",), writes=(f"ea_s{hp}",))
                    tt(cmsc[hp], bmcm_t.ap()[:, 512 + cc * 256:][:, 0:256], ea_s[hp],
                       ALU.mult, reads=("bmcm", f"ea_s{hp}"), writes=(f"cmsc{hp}",),
                       eng="gpsimd")
                    first = (h % 2 == 0)
                    for sb_i in range(2):
                        mm(psb[ybank].ap()[(h % 2) * 64:(h % 2) * 64 + 64, 0:256],
                           hsTM_t.ap()[:, sb_i * 512 + h8 * 64:][:, 0:64],
                           mth[mtb[sb_i]], sb_i == 0, False,
                           reads=("hsTM", f"mth{mtb[sb_i]}"), writes=(f"ps{ybank}",),
                           inc=False)
                    mm(psb[ybank].ap()[(h % 2) * 64:(h % 2) * 64 + 64, 0:256],
                       s16g.ap()[:, h8 * 64:(h8 + 1) * 64], cmsc[hp], False,
                       True, reads=("s16g", f"cmsc{hp}"), writes=(f"ps{ybank}",))
                    if h % 2 == 1:
                        stt(y_tile(pj), hs_slot(pj, c), sb_dexp.ap()[:, pj:pj + 1],
                            psf(ybank)[:, 0:256], ALU.mult, ALU.add,
                            reads=(f"hs{c}_{pj}", "sb_dexp", f"ps{ybank}"),
                            writes=(f"y{pj}",))
            if _KSUB < 8:
                continue
            for h8 in range(8):
                for sb_i in range(2):
                    mm(psf(0)[:, h8 * 64:(h8 + 1) * 64],
                       bmTM.ap()[:, sb_i * 128:(sb_i + 1) * 128],
                       hsW_t.ap()[:, sb_i * 512 + h8 * 64:][:, 0:64],
                       h8 == 0 and sb_i == 0, h8 == 7 and sb_i == 1,
                       reads=("bmTM", "hsW"), writes=("ps0",),
                       inc=(h8 == 7 and sb_i == 1))
            if _KSUB < 9:
                continue
            mm(psf(1), ones1b.ap(),
               rowd.ap()[0:1, g * 8:(g + 1) * 8].unsqueeze(2).to_broadcast([1, 8, 64]),
               True, True, reads=("ones1b", "rowd"), writes=("ps1",))
            tt(S[:, g * 512:(g + 1) * 512], S[:, g * 512:(g + 1) * 512], psf(1),
               ALU.mult, reads=("ps1", "s16g"), writes=("S",))
            tt(S[:, g * 512:(g + 1) * 512], S[:, g * 512:(g + 1) * 512], psf(0),
               ALU.add, reads=("ps0",), writes=("S",))

    def epilogue(c):
        for ft in range(NFT):
            if ft % 2 == 0:
                wbuf = (ft // 2) % 2
                for ft2 in (ft, ft + 1):
                    dma("gpsimd", wab[wbuf][:, (ft2 % 2) * NK * 128:][:, 0:NK * 128],
                        wb_t[ft2, :, :], writes=(f"wab{wbuf}",))
            bi = ft % 2
            wbuf = (ft // 2) % 2
            for k in range(NK):
                mm(psf(bi, 256), wab[wbuf][:, (ft % 2) * NK * 128 + k * 128:][:, 0:128],
                   xn[k][:, c * 256:(c + 1) * 256], k == 0, k == NK - 1,
                   reads=(f"wab{wbuf}", f"xn{k}"), writes=(f"ps{bi}",),
                   inc=(k == NK - 1))
            act(sg_tile(ft), psf(bi, 256), AF.Silu, bias=sb_bg.ap()[:, ft:ft + 1],
                reads=(f"ps{bi}", "sb_bg"), writes=(f"sg{ft % 2}",))
            tt(y_tile(ft), y_tile(ft), sg_tile(ft), ALU.mult,
               reads=(f"y{ft}", f"sg{ft % 2}"), writes=(f"y{ft}",))
            act(sqroll[ft % 2], y_tile(ft), AF.Square,
                reads=(f"y{ft}",), writes=(f"sq{ft % 2}",))
            mm(psb[4].ap()[0:1, 0:256], onescb.ap(), sqroll[ft % 2], ft == 0,
               ft == NFT - 1, reads=(f"sq{ft % 2}", "onescb"), writes=("ps4",),
               inc=(ft == NFT - 1))
        act(rst_row.ap(), psb[4].ap()[0:1, 0:256], AF.Sqrt, bias=epsc.ap()[0:1, 0:1],
            scale=1.0 / INTER, reads=("ps4", "epsc"), writes=("rstrow",))
        pr.op("vector", lambda e: e.reciprocal(rst_row.ap(), rst_row.ap()),
              reads=(), writes=("rstrow",))
        cp(rst16.ap(), rst_row.ap(), reads=("rstrow",), writes=("rst16",))
        mm(psf(5, 256), ones1b.ap(), rst16.ap(), True, True,
           reads=("ones1b", "rst16"), writes=("ps5",))
        for ft in range(NFT):
            stt(hs_slot(ft, c), y_tile(ft), sb_nw.ap()[:, ft:ft + 1], psf(5, 256),
                ALU.mult, ALU.mult, reads=(f"y{ft}", "sb_nw", "ps5"),
                writes=(f"hs{c}_{ft}",))

    def halo_mask():
        ts(S, S, sb_hm.ap(), None, ALU.mult, reads=("sb_hm",), writes=("S",))
        ts(prev3_t.ap(), prev3_t.ap(), sb_hm.ap(), None, ALU.mult,
           reads=("sb_hm",), writes=("prev3",))

    mset(S, 0.0, writes=("S",))
    if _TRUNC >= 4:
        for pairi in range(2):
            scanprep("h", pairi, xnh, XNH)
            if _TRUNC >= 5:
                for cc in range(2):
                    scancore(pairi * 2 + cc, cc, out=False)
        halo_mask()
    if _TRUNC >= 6:
        for pairi in range(2):
            scanprep("m", pairi, xn, XN)
            for cc in range(2):
                c = pairi * 2 + cc
                if _TRUNC >= 7:
                    scancore(c, cc, out=True)
                if _TRUNC >= 8:
                    epilogue(c)

    pr.fence()
    SC.pop()
    SC.pop()
    SC.push()
    wob = [SC.bf16(2 * NK * 128) for _ in range(2)]
    def ps_op(dt_i, c):
        return psb[dt_i // 2].ap()[:, (dt_i % 2) * 256:(dt_i % 2) * 256 + 256]
    for c in range(4 if _TRUNC >= 9 else 0):
        for ft in range(NFT):
            if ft % 2 == 0:
                wbuf = (ft // 2) % 2
                for ft2 in (ft, ft + 1):
                    dma("gpsimd", wob[wbuf][:, (ft2 % 2) * NK * 128:][:, 0:NK * 128],
                        wo_t[ft2, :, :], writes=(f"wob{wbuf}",))
            wbuf = (ft // 2) % 2
            for dt_i in range(16):
                mm(ps_op(dt_i, c),
                   wob[wbuf][:, (ft % 2) * NK * 128 + dt_i * 128:][:, 0:128],
                   hs_slot(ft, c), ft == 0 and dt_i % 2 == 0,
                   ft == NFT - 1 and dt_i % 2 == 1,
                   reads=(f"wob{wbuf}", f"hs{c}_{ft}"),
                   writes=(f"ps{dt_i // 2}",),
                   inc=(dt_i == 15))
        for dt_i in range(16):
            ts(acc_slot(dt_i, c), ps_op(dt_i, c), sb_bo.ap()[:, dt_i:dt_i + 1], None,
               ALU.add, reads=(f"ps{dt_i // 2}", "sb_bo"), writes=(f"acc{dt_i}",))

    pr.fence()
    SC.pop()
    SC.push()
    w13b = [SC.bf16(2 * NK * 128) for _ in range(4)]
    h2reg = SC.bf16(16 * TPC)
    w2g = [SC.bf16(2048) for _ in range(8)]
    be_ap = SC.bf16(TPC)
    ffn_s1 = SC.bf16(TPC)

    class _A2:
        def __init__(self, ap_):
            self._ap = ap_
        def ap(self):
            return self._ap
    be_s = _A2(be_ap)

    def h2_tile(i16):
        return h2reg[:, i16 * TPC:(i16 + 1) * TPC]

    for e_i in range(8 if _TRUNC >= 10 else 0):
        for h2_ in range(2):
            mm(psf(4 + h2_, 512),
               id64b.ap()[0:8, e_i:e_i + 1].to_broadcast([8, 128]),
               wfm.ap()[0:8, h2_ * 512:(h2_ + 1) * 512], True, True,
               reads=("id64b", "wfm"), writes=(f"ps{4 + h2_}",))
            cp(be_s.ap()[:, h2_ * 512:(h2_ + 1) * 512], psf(4 + h2_, 512),
               reads=(f"ps{4 + h2_}",), writes=("be",))
        for ih in range(2):
            for it16 in range(16):
                it = ih * 16 + it16
                if it16 % 2 == 0:
                    wb1 = (it16 // 2) % 2
                    for d2 in range(2):
                        dma("gpsimd",
                            w13b[wb1][:, d2 * NK * 128:][:, 0:NK * 128],
                            w1_t[e_i, it + d2, :, :], writes=(f"w13b{wb1}",))
                        dma("gpsimd",
                            w13b[2 + wb1][:, d2 * NK * 128:][:, 0:NK * 128],
                            w3_t[e_i, it + d2, :, :], writes=(f"w13b{2 + wb1}",))
                wb1 = (it16 // 2) % 2
                woff = (it16 % 2) * NK * 128
                pb1 = (it16 % 2) * 2
                pb3 = 4 + (it16 % 2) * 2
                for k in range(NK):
                    for hf in range(2):
                        mm(psf(pb1 + hf), w13b[wb1][:, woff + k * 128:][:, 0:128],
                           xn[k][:, hf * 512:(hf + 1) * 512], k == 0, k == NK - 1,
                           reads=(f"w13b{wb1}", f"xn{k}"), writes=(f"ps{pb1 + hf}",),
                           inc=(k == NK - 1))
                        mm(psf(pb3 + hf), w13b[2 + wb1][:, woff + k * 128:][:, 0:128],
                           xn[k][:, hf * 512:(hf + 1) * 512], k == 0, k == NK - 1,
                           reads=(f"w13b{2 + wb1}", f"xn{k}"), writes=(f"ps{pb3 + hf}",),
                           inc=(k == NK - 1))
                for hf in range(2):
                    act(ffn_s1[:, hf * 512:(hf + 1) * 512],
                        psf(pb1 + hf), AF.Silu,
                        reads=(f"ps{pb1 + hf}",), writes=("f_s1",))
                    tt(h2_tile(it16)[:, hf * 512:(hf + 1) * 512],
                       psf(pb3 + hf), be_s.ap()[:, hf * 512:(hf + 1) * 512], ALU.mult,
                       reads=(f"ps{pb3 + hf}", "be"), writes=(f"h2_{it16}",))
                tt(h2_tile(it16), h2_tile(it16), ffn_s1, ALU.mult,
                   reads=("f_s1",), writes=(f"h2_{it16}",))
            for itg in range(2):
                for j in range(8):
                    it = ih * 16 + itg * 8 + j
                    dma("gpsimd", w2g[j], w2_n[e_i, it * 128:(it + 1) * 128, :],
                        writes=(f"w2g{j}",))
                for dt_i in range(16):
                    pbk = dt_i % 2
                    for j in range(8):
                        it16 = itg * 8 + j
                        for hf in range(2):
                            mm(psf(pbk * 2 + hf),
                               w2g[j][:, dt_i * 128:(dt_i + 1) * 128],
                               h2_tile(it16)[:, hf * 512:(hf + 1) * 512],
                               j == 0, j == 7,
                               reads=(f"w2g{j}", f"h2_{it16}"),
                               writes=(f"ps{pbk * 2 + hf}",),
                               inc=(j == 7 and hf == 1))
                    for hf in range(2):
                        tt(arena[:, dt_i * 1024 + hf * 512:][:, 0:512],
                           arena[:, dt_i * 1024 + hf * 512:][:, 0:512],
                           psf(pbk * 2 + hf), ALU.add,
                           reads=(f"ps{pbk * 2 + hf}", f"acc{dt_i}"),
                           writes=(f"acc{dt_i}",))

    pr.fence()
    SC.pop()
    SC.push()
    xf = [SC.f32(D) for _ in range(2)]
    of = [SC.f32(D) for _ in range(2)]
    for tti in range(8):
        bi = tti % 2
        dma("sync", xf[bi], x_main[tti * 128:(tti + 1) * 128, :], writes=(f"xf{bi}",))
        for g4 in range(4):
            bank = 4 * (g4 % 2)
            for j in range(4):
                dt_i = g4 * 4 + j
                tr(psb[bank + j].ap()[:, 0:128],
                   arena[:, dt_i * 1024 + tti * 128:][:, 0:128],
                   id128fs.ap(),
                   reads=(f"acc{dt_i}", "id128fs"), writes=(f"ps{bank + j}",),
                   inc=(j == 3))
            for j in range(4):
                dt_i = g4 * 4 + j
                tt(of[bi][:, dt_i * 128:(dt_i + 1) * 128], xf[bi][:, dt_i * 128:(dt_i + 1) * 128],
                   psb[bank + j].ap()[:, 0:128], ALU.add,
                   reads=(f"ps{bank + j}", f"xf{bi}"), writes=(f"of{bi}",))
        dma("sync", o_out[tti * 128:(tti + 1) * 128, :], of[bi],
            reads=(f"of{bi}",), writes=(f"ost{bi}",))
    pr._wait("sync", pr.last_w.get("ost0"))
    pr._wait("sync", pr.last_w.get("ost1"))

    pr.replay()
    return nc


_NC_CACHE = {}

def _prep_static(inputs):
    f = np.float32
    W = np.ascontiguousarray
    ip_w = np.asarray(inputs["L0_attn_in_proj_w"], f)
    ip_b = np.asarray(inputs["L0_attn_in_proj_b"], f)
    wa = ip_w[:, INTER:INTER + CONV]
    wadt = ip_w[:, INTER + CONV:]
    wb = ip_w[:, :INTER]
    wo = np.asarray(inputs["L0_attn_out_proj_w"], f)
    w1 = np.asarray(inputs["L0_ffn_w1"], f)
    w3 = np.asarray(inputs["L0_ffn_w3"], f)
    w2 = np.asarray(inputs["L0_ffn_w2"], f)

    def tile_cols(w, nct):
        return W(w.reshape(NK, 128, nct, 128).transpose(2, 1, 0, 3)
                 .reshape(nct, 128, NK * 128))

    d = {}
    d["wa_t"] = tile_cols(wa, NCT)
    d["wa_dt"] = W(wadt.reshape(NK, 128, 64).transpose(1, 0, 2).reshape(128, NK * 64))
    d["wb_t"] = tile_cols(wb, NFT)
    d["wo_t"] = W(wo.reshape(NFT, 128, D))
    d["w1_t"] = W(np.stack([tile_cols(w1[e], NFT) for e in range(8)]))
    d["w3_t"] = W(np.stack([tile_cols(w3[e], NFT) for e in range(8)]))
    d["w2_n"] = W(w2)
    d["wg_t"] = W(np.asarray(inputs["L0_ffn_w_gate"], f).reshape(NK, 128, 8)
                  .transpose(1, 0, 2).reshape(128, NK * 8))
    cw = np.asarray(inputs["L0_attn_conv_w"], f)
    d["conv_w_t"] = W(cw.reshape(NCT, 128, KCV).transpose(1, 0, 2).reshape(128, NCT * KCV))
    d["ba_conv"] = W(ip_b[INTER:INTER + CONV].reshape(NCT, 128).T)
    d["ba_dtc"] = W(ip_b[INTER + CONV:].reshape(64, 1))
    d["bg_t"] = W(ip_b[:INTER].reshape(NFT, 128).T)
    d["bo_t"] = W(np.asarray(inputs["L0_attn_out_proj_b"], f).reshape(16, 128).T)
    d["dt_bias"] = W(np.asarray(inputs["L0_attn_dt_bias"], f).reshape(64, 1))
    d["a_log"] = W(np.asarray(inputs["L0_attn_A_log"], f).reshape(64, 1))
    d["d_exp"] = W(np.repeat(np.asarray(inputs["L0_attn_D"], f), 64)
                   .reshape(32, 128).T)
    d["nw_t"] = W(np.asarray(inputs["L0_attn_norm_weight"], f).reshape(32, 128).T)
    d["id128f"] = np.eye(128, dtype=f)
    d["id64f"] = np.eye(64, dtype=f)
    d["id64d_in"] = np.concatenate([np.eye(64, dtype=f)] * 2, axis=0)
    d["diagneg"] = np.eye(128, dtype=f) * -1.0e30
    ls = np.zeros((128, 2 * LC), f)
    for sb_i in range(2):
        srow = np.arange(128)[:, None] + sb_i * 128
        lcol = np.arange(LC)[None, :]
        ls[:, sb_i * LC:(sb_i + 1) * LC] = (lcol < srow).astype(f)
    d["lstrict"] = ls
    mk = np.zeros((128, 2 * LC), f)
    for sb_i in range(2):
        srow = np.arange(128)[:, None] + sb_i * 128
        lcol = np.arange(LC)[None, :]
        mk[:, sb_i * LC:(sb_i + 1) * LC] = np.where(lcol >= srow, 0.0, -1.0e30)
    d["maskc"] = mk
    return d


def kernel(**inputs):
    x = np.asarray(inputs["L0_attn_hidden_states"], np.float32)
    stat = _prep_static(inputs)
    nc = build_nc()
    in_maps = []
    for core in range(8):
        b, half = core // 2, core % 2
        m = dict(stat)
        m["x_main"] = np.ascontiguousarray(x[b, half * TPC:(half + 1) * TPC])
        m["x_halo"] = np.ascontiguousarray(x[b, 0:TPC])
        m["hmask"] = np.full((128, 1), float(half), np.float32)
        in_maps.append(m)
    res = run_bass_kernel_spmd(nc, in_maps, core_ids=list(range(8)))
    out = np.empty((4, 2048, D), np.float32)
    for core in range(8):
        b, half = core // 2, core % 2
        out[b, half * TPC:(half + 1) * TPC] = res.results[core]["o_out"]
    return out


if __name__ == "__main__":
    pass


# version 06
# speedup vs baseline: 1.1294x; 1.1294x over previous
import os
import numpy as np
import ml_dtypes
from contextlib import ExitStack

_TRUNC = int(os.environ.get("KTRUNC", "99"))
_KSUB = int(os.environ.get("KSUB", "99"))

import concourse.bass as bass
import concourse.mybir as mybir
from concourse.bass_utils import run_bass_kernel_spmd

F32 = mybir.dt.float32
BF16 = mybir.dt.bfloat16
ALU = mybir.AluOpType
AF = mybir.ActivationFunctionType
AX = mybir.AxisListType

D = 2048
TPC = 1024
H = 64
LC = 256
KCV = 4
INTER = 4096
CONV = 4352
NCT = 34
NK = 16
NFT = 32
EPS = 1e-5

ENGS = ("sync", "gpsimd", "vector", "scalar", "tensor")
_PEND = object()


class Prog:

    def __init__(self, nc, es):
        self.nc = nc
        self.items = {e: [] for e in ENGS}
        self.cnt = {}
        self.sems = {}
        self.waited = {e: {} for e in ENGS}
        self.es = es
        self.last_w = {}
        self.last_r = {}
        self.esem = {"sync": "ld", "gpsimd": "gp", "vector": "dv",
                     "scalar": "ac", "tensor": "pe"}
        self.pending = {e: [] for e in ENGS}

    def sem(self, name):
        if name not in self.sems:
            self.sems[name] = self.es.enter_context(self.nc.semaphore(name))
            self.cnt[name] = 0
        return self.sems[name]

    def _wait(self, eng, tok):
        if tok is None:
            return
        if isinstance(tok, tuple) and tok[0] is _PEND:
            if tok[1] == eng:
                return
            raise AssertionError(
                f"{eng} depends on un-inc'd op of {tok[1]} -- add inc=True there")
        s, v = tok
        if self.waited[eng].get(s, 0) >= v:
            return
        self.waited[eng][s] = v
        self.sem(s)
        self.items[eng].append(("wait", s, v))

    def op(self, eng, fn, reads=(), writes=(), inc=True, n=1):
        for k in reads:
            self._wait(eng, self.last_w.get(k))
        for k in writes:
            self._wait(eng, self.last_w.get(k))
            for t in self.last_r.get(k, ()):
                self._wait(eng, t)
        tok = None
        if inc:
            s = self.esem[eng]
            self.sem(s)
            self.cnt[s] += n
            tok = (s, self.cnt[s])
            for ent in self.pending[eng]:
                kind, key, idx = ent
                if kind == "w":
                    if self.last_w.get(key) is _pendmark(eng):
                        self.last_w[key] = tok
                else:
                    lr = self.last_r.get(key, [])
                    for i2, t2 in enumerate(lr):
                        if t2 is _pendmark(eng):
                            lr[i2] = tok
            self.pending[eng] = []
        self.items[eng].append(("op", fn, self.esem[eng] if inc else None, n))
        mark = tok if inc else _pendmark(eng)
        for k in writes:
            self.last_w[k] = mark
            self.last_r[k] = []
        for k in reads:
            self.last_r.setdefault(k, []).append(mark)
        if not inc:
            for k in writes:
                self.pending[eng].append(("w", k, None))
            for k in reads:
                self.pending[eng].append(("r", k, None))
        return tok

    def fence(self):
        for eng in ENGS:
            for s, c in list(self.cnt.items()):
                if c > 0:
                    self._wait(eng, (s, c))

    def replay(self):
        with self.nc.Block() as block:
            for eng in ENGS:
                def runner(e, eng=eng):
                    for item in self.items[eng]:
                        if item[0] == "op":
                            _, fn, incname, n = item
                            inst = fn(e)
                            if incname is not None:
                                inst.then_inc(self.sems[incname], n)
                        else:
                            _, s, v = item
                            e.wait_ge(self.sems[s], v)
                getattr(block, eng)(runner)


_pendmarks = {}
def _pendmark(eng):
    if eng not in _pendmarks:
        _pendmarks[eng] = (_PEND, eng)
    return _pendmarks[eng]


class Bump:
    def __init__(self, t, cap):
        self.t, self.cap, self.top, self.marks = t, cap, 0, []
    def push(self):
        self.marks.append(self.top)
    def pop(self):
        self.top = self.marks.pop()
    def f32(self, n, parts=128):
        o = self.top
        assert o + n <= self.cap, f"scratch overflow {o}+{n}>{self.cap}"
        self.top += n
        return self.t.ap()[0:parts, o:o + n]
    def bf16(self, n, parts=128):
        assert n % 2 == 0
        o = self.top
        assert o + n // 2 <= self.cap, f"scratch overflow {o}+{n // 2}>{self.cap}"
        self.top += n // 2
        return self.t.ap().bitcast(BF16)[0:parts, 2 * o:2 * o + n]


def build_nc():
    nc = bass.Bass("TRN2", target_bir_lowering=False)
    es = ExitStack()
    pr = Prog(nc, es)

    def din(name, shape):
        return nc.dram_tensor(name, list(shape), F32, kind="ExternalInput").ap()

    x_main = din("x_main", (TPC, D))
    x_halo = din("x_halo", (TPC, D))
    wa_t = din("wa_t", (NCT, 128, NK * 128))
    wa_dt = din("wa_dt", (128, NK * 64))
    wb_t = din("wb_t", (NFT, 128, NK * 128))
    wo_t = din("wo_t", (NFT, 128, NK * 128))
    w1_t = din("w1_t", (8, NFT, 128, NK * 128))
    w3_t = din("w3_t", (8, NFT, 128, NK * 128))
    w2_n = din("w2_n", (8, INTER, D))
    wg_t = din("wg_t", (128, NK * 8))
    conv_w_t = din("conv_w_t", (128, NCT * KCV))
    ba_conv = din("ba_conv", (128, NCT))
    ba_dtc = din("ba_dtc", (64, 1))
    bg_t = din("bg_t", (128, NFT))
    bo_t = din("bo_t", (128, 16))
    dt_bias = din("dt_bias", (64, 1))
    a_log = din("a_log", (64, 1))
    d_exp = din("d_exp", (128, 32))
    nw_t = din("nw_t", (128, 32))
    hmask = din("hmask", (128, 1))
    id128f = din("id128f", (128, 128))
    id64f = din("id64f", (64, 64))
    id64d_in = din("id64d_in", (128, 64))
    maskc = din("maskc", (128, 2 * LC))
    diagneg = din("diagneg", (128, 128))
    lstrict = din("lstrict", (128, 2 * LC))
    o_out = nc.dram_tensor("o_out", [TPC, D], F32, kind="ExternalOutput").ap()

    def sbt(name, shape, dtype):
        return es.enter_context(nc.sbuf_tensor(name, list(shape), dtype))

    xn_t = sbt("xn_t", (128, NK * TPC), BF16)
    xn = [xn_t.ap()[:, k * TPC:(k + 1) * TPC] for k in range(NK)]
    arena_t = sbt("arena_t", (128, 16384), F32)
    arena = arena_t.ap()
    arena_bf = arena_t.ap().bitcast(BF16)

    def hs_slot(ft, c):
        return arena_bf[:, (ft // 2) * 2048 + c * 512 + (ft % 2) * 256:][:, 0:256]

    def acc_slot(dt_i, c):
        return arena[:, dt_i * 1024 + c * 256:][:, 0:256]

    def acc_row(dt_i):
        return arena[:, dt_i * 1024:(dt_i + 1) * 1024]

    prev3_t = sbt("prev3_t", (128, (NCT + 1) * 3), F32)
    id128b = sbt("id128b", (128, 128), BF16)
    id128fs = sbt("id128fs", (128, 128), F32)
    id64b = sbt("id64b", (64, 64), BF16)
    id64d = sbt("id64d", (128, 64), BF16)
    diagn_b = sbt("diagn_b", (128, 128), BF16)
    lstr_b = sbt("lstr_b", (128, 2 * LC), BF16)
    id64fs = sbt("id64fs", (64, 64), F32)
    ones1b = sbt("ones1b", (1, 128), BF16)
    onescb = sbt("onescb", (128, 1), BF16)
    mks_t = sbt("mks_t", (128, 2 * LC), F32)
    convw = sbt("convw", (128, NCT * KCV), F32)
    sb_ba = sbt("sb_ba", (128, NCT), F32)
    sb_badt = sbt("sb_badt", (64, 1), F32)
    sb_bg = sbt("sb_bg", (128, NFT), F32)
    sb_bo = sbt("sb_bo", (128, 16), F32)
    sb_dtb = sbt("sb_dtb", (64, 1), F32)
    sb_alog = sbt("sb_alog", (64, 1), F32)
    aneg = sbt("aneg", (64, 1), F32)
    badt_comb = sbt("badt_comb", (64, 1), F32)
    sb_dexp = sbt("sb_dexp", (128, 32), F32)
    sb_nw = sbt("sb_nw", (128, 32), F32)
    sb_hm = sbt("sb_hm", (128, 1), F32)
    sb_wg = sbt("sb_wg", (128, NK * 8), BF16)
    stat = sbt("stat", (128, 8), F32)
    wfm = sbt("wfm", (8, TPC), BF16)
    r8s = sbt("r8s", (128, 64), F32)
    onesr = sbt("onesr", (64, LC), F32)
    epsc = sbt("epsc", (128, 1), F32)

    SCCAP = 26032
    sc_t = sbt("sc_t", (128, SCCAP), F32)
    SC = Bump(sc_t, SCCAP)

    psb = [es.enter_context(nc.psum_tensor(f"psb{i}", [128, 512], F32)) for i in range(8)]
    def psf(i, n=512, p=128):
        return psb[i].ap()[0:p, 0:n]
    def psh(i, n=1024, p=128):
        return psb[i].ap().bitcast(BF16)[0:p, 0:n]

    NDMASEM = 16
    dma_ctr = {"sync": 0, "gpsimd": 0}

    def dma(q, dst, src, reads=(), writes=(), inc=True):
        i = dma_ctr[q]
        dma_ctr[q] = i + 1
        semname = f"dq_{q}_{i % NDMASEM}"
        pr.sem(semname)
        if i >= NDMASEM:
            pr._wait(q, (semname, 16 * (i // NDMASEM)))
        for k in reads:
            pr._wait(q, pr.last_w.get(k))
        for k in writes:
            pr._wait(q, pr.last_w.get(k))
            for t in pr.last_r.get(k, ()):
                pr._wait(q, t)
        pr.cnt[semname] += 16
        tok = (semname, pr.cnt[semname])
        pr.items[q].append(("op", lambda e, d=dst, s=src: e.dma_start(out=d, in_=s),
                            semname, 16))
        for k in writes:
            pr.last_w[k] = tok
            pr.last_r[k] = []
        for k in reads:
            pr.last_r.setdefault(k, []).append(tok)
        return tok

    def mm(out, lhsT, rhs, start, stop, reads=(), writes=(), inc=True):
        return pr.op("tensor", lambda e, o=out, l=lhsT, r=rhs, a=start, b=stop:
                     e.matmul(o, l, r, start=a, stop=b),
                     reads=reads, writes=writes, inc=inc)

    def tr(out, in_, ident, reads=(), writes=(), inc=True):
        return pr.op("tensor", lambda e, o=out, i=in_, d=ident: e.transpose(o, i, d),
                     reads=reads, writes=writes, inc=inc)

    def tt(out, a, b, opk, reads=(), writes=(), eng="vector"):
        return pr.op(eng, lambda e, o=out, x=a, y=b, k=opk: e.tensor_tensor(o, x, y, op=k),
                     reads=reads, writes=writes)

    def ts(out, a, s1, s2, op0, op1=None, reads=(), writes=(), accum=None, eng="vector"):
        if op1 is None:
            return pr.op(eng, lambda e, o=out, x=a, s=s1: e.tensor_scalar(
                o, x, s, None, op0=op0), reads=reads, writes=writes)
        return pr.op(eng, lambda e, o=out, x=a, s=s1, s2=s2: e.tensor_scalar(
            o, x, s, s2, op0=op0, op1=op1), reads=reads, writes=writes)

    def stt(out, a, s1, b, op0, op1, reads=(), writes=()):
        return pr.op("vector", lambda e, o=out, x=a, s=s1, y=b: e.scalar_tensor_tensor(
            o, x, s, y, op0=op0, op1=op1), reads=reads, writes=writes)

    def act(out, in_, func, bias=0.0, scale=1.0, accum=None, reads=(), writes=()):
        return pr.op("scalar", lambda e, o=out, i=in_, f=func, b=bias, s=scale, a=accum:
                     e.activation(o, i, f, bias=b, scale=s, accum_out=a),
                     reads=reads, writes=writes)

    def cp(out, in_, reads=(), writes=(), eng="vector"):
        return pr.op(eng, lambda e, o=out, i=in_: e.tensor_copy(o, i),
                     reads=reads, writes=writes)

    def mset(ap_, v, reads=(), writes=()):
        return pr.op("vector", lambda e, o=ap_, v=v: e.memset(o, v),
                     reads=reads, writes=writes)

    for dst, src, key in [
        (convw.ap(), conv_w_t, "convw"), (sb_ba.ap(), ba_conv, "sb_ba"),
        (sb_badt.ap(), ba_dtc, "sb_badt"), (sb_bg.ap(), bg_t, "sb_bg"),
        (sb_bo.ap(), bo_t, "sb_bo"), (sb_dtb.ap(), dt_bias, "sb_dtb"),
        (sb_alog.ap(), a_log, "sb_alog"), (sb_dexp.ap(), d_exp, "sb_dexp"),
        (sb_nw.ap(), nw_t, "sb_nw"), (sb_hm.ap(), hmask, "sb_hm"),
        (id128fs.ap(), id128f, "id128fs"), (id64fs.ap(), id64f, "id64fs"),
        (mks_t.ap(), maskc, "mks"),
    ]:
        dma("sync", dst, src, writes=(key,))
    for dst, src, key in [(id128b.ap(), id128f, "id128b"), (id64b.ap(), id64f, "id64b"),
                          (id64d.ap(), id64d_in, "id64d"), (sb_wg.ap(), wg_t, "sb_wg"),
                          (diagn_b.ap(), diagneg, "diagn"), (lstr_b.ap(), lstrict, "lstr")]:
        dma("gpsimd", dst, src, writes=(key,))
    mset(ones1b.ap(), 1.0, writes=("ones1b",))
    mset(onescb.ap(), 1.0, writes=("onescb",))
    mset(onesr.ap(), 1.0, writes=("onesr",))
    mset(prev3_t.ap(), 0.0, writes=("prev3",))
    mset(epsc.ap(), EPS, writes=("epsc",))
    act(aneg.ap(), sb_alog.ap(), AF.Exp, reads=("sb_alog",), writes=("aneg",))
    tt(badt_comb.ap(), sb_badt.ap(), sb_dtb.ap(), ALU.add,
       reads=("sb_badt", "sb_dtb"), writes=("badtc",))
    ts(aneg.ap(), aneg.ap(), -1.0, None, ALU.mult, reads=(), writes=("aneg",))

    SC.push()
    regA = SC.f32(6700)
    ln_a = regA[:, 0:D]
    ln_b = regA[:, D:2 * D]
    ln_q = regA[:, 2 * D:3 * D]
    yreg = SC.f32(8192)
    xnh = [yreg.bitcast(BF16)[:, k * TPC:(k + 1) * TPC] for k in range(NK)]

    def emit_ln(x_hbm, xout, xkey):
        for i in range(8):
            buf = ln_a if i % 2 == 0 else ln_b
            lq = ln_q[:, (i % 2) * 1024:(i % 2) * 1024 + D] if False else ln_q
            bk = f"lnbuf{i % 2}"
            so = (i % 2) * 4
            dma("sync", buf, x_hbm[i * 128:(i + 1) * 128, :], writes=(bk,))
            pr.op("vector", lambda e, b=buf, so=so: e.reduce_sum(
                stat.ap()[:, so:so + 1], b, axis=AX.X, negate=True),
                reads=(bk,), writes=(f"st0{so}",))
            ts(stat.ap()[:, so:so + 1], stat.ap()[:, so:so + 1], 1.0 / D, None,
               ALU.mult, reads=(), writes=(f"st0{so}",))
            ts(ln_q, buf, stat.ap()[:, so:so + 1], None, ALU.add,
               reads=(bk, f"st0{so}"), writes=("lnq",))
            act(buf, ln_q, AF.Square, accum=stat.ap()[:, so + 1:so + 2],
                reads=("lnq",), writes=(bk, f"st1{so}"))
            act(stat.ap()[:, so + 2:so + 3], stat.ap()[:, so + 1:so + 2], AF.Sqrt,
                bias=epsc.ap(), scale=1.0 / D,
                reads=(f"st1{so}", "epsc"), writes=(f"st2{so}",))
            pr.op("vector", lambda e, so=so: e.reciprocal(
                stat.ap()[:, so + 3:so + 4], stat.ap()[:, so + 2:so + 3]),
                reads=(f"st2{so}",), writes=(f"st3{so}",))
            ts(buf.bitcast(BF16)[:, 0:D], ln_q, stat.ap()[:, so + 3:so + 4], None,
               ALU.mult, reads=("lnq", f"st3{so}"), writes=(bk,))
            for g4 in range(4):
                bank = g4 % 2
                tks = None
                for j in range(4):
                    k = g4 * 4 + j
                    tks = tr(psh(bank)[:, j * 128:(j + 1) * 128],
                             buf.bitcast(BF16)[:, k * 128:(k + 1) * 128], id128b.ap(),
                             reads=(bk, "id128b"), writes=(f"ps{bank}",),
                             inc=(j == 3))
                for j in range(4):
                    k = g4 * 4 + j
                    cp(xout[k][:, i * 128:(i + 1) * 128],
                       psh(bank)[:, j * 128:(j + 1) * 128],
                       reads=(f"ps{bank}",), writes=(f"{xkey}{k}",))

    emit_ln(x_main, xn, "xn")
    if _TRUNC >= 2:
        emit_ln(x_halo, xnh, "xnh")
    XN = tuple(f"xn{k}" for k in range(NK))
    XNH = tuple(f"xnh{k}" for k in range(NK))

    for i in range(8 if _TRUNC >= 3 else 0):
        bank = 2 + (i % 2)
        for k in range(NK):
            mm(psf(bank, 8), xn[k][:, i * 128:(i + 1) * 128],
               sb_wg.ap()[:, k * 8:(k + 1) * 8], k == 0, k == NK - 1,
               reads=(f"xn{k}", "sb_wg"), writes=(f"ps{bank}",), inc=(k == NK - 1))
        act(r8s.ap()[:, 0:8], psf(bank, 8), AF.Tanh, scale=1.0 / 30.0,
            reads=(f"ps{bank}",), writes=("r8a",))
        pr.op("vector", lambda e: e.reduce_max(
            r8s.ap()[:, 8:9], r8s.ap()[:, 0:8], axis=AX.X, negate=True),
            reads=("r8a",), writes=("r8m",))
        act(r8s.ap()[:, 16:24], r8s.ap()[:, 0:8], AF.Exp, bias=r8s.ap()[:, 8:9],
            scale=30.0, accum=r8s.ap()[:, 9:10], reads=("r8a", "r8m"),
            writes=("r8e", "r8sum"))
        pr.op("vector", lambda e: e.reciprocal(r8s.ap()[:, 10:11], r8s.ap()[:, 9:10]),
              reads=("r8sum",), writes=("r8rc",))
        ts(r8s.ap()[:, 24:32], r8s.ap()[:, 16:24], r8s.ap()[:, 10:11], None, ALU.mult,
           reads=("r8e", "r8rc"), writes=("r8p",))
        pr.op("vector", lambda e: e.max(out=r8s.ap()[:, 32:40], in_=r8s.ap()[:, 24:32]),
              reads=("r8p",), writes=("r8t",))
        ts(r8s.ap()[:, 40:48], r8s.ap()[:, 24:32], r8s.ap()[:, 33:34], None, ALU.is_ge,
           reads=("r8p", "r8t"), writes=("r8k",))
        tt(r8s.ap()[:, 48:56], r8s.ap()[:, 24:32], r8s.ap()[:, 40:48], ALU.mult,
           reads=("r8p", "r8k"), writes=("r8w",))
        tr(psb[bank].ap()[0:8, 0:128], r8s.ap()[:, 48:56], id128fs.ap(),
           reads=("r8w", "id128fs"), writes=(f"ps{bank}",))
        cp(wfm.ap()[0:8, i * 128:(i + 1) * 128], psb[bank].ap()[0:8, 0:128],
           reads=(f"ps{bank}",), writes=("wfm",))

    pr.fence()
    SC.push()
    proj_c = [regA[:, i * 520:i * 520 + 516] for i in range(2)]
    cacc_b = [regA[:, 1040 + i * 512:1040 + (i + 1) * 512] for i in range(2)]
    wab = [regA.bitcast(BF16)[:, 2 * (2064 + i * 2048):][:, 0:4096] for i in range(2)]
    wdtb = regA.bitcast(BF16)[:, 2 * 6160:][:, 0:1024]
    S = SC.f32(INTER)
    bmcm_ap = SC.bf16(1024)
    dtb_ap = SC.f32(512, parts=64)
    acum_ = SC.f32(LC, parts=64)
    ea16_ = SC.bf16(LC, parts=64)
    wsc_ = SC.f32(LC, parts=64)
    dtc_ = SC.f32(LC, parts=64)
    abt_ = SC.f32(LC, parts=64)
    negacumT_ = SC.f32(128)
    dtT_ = SC.f32(128)
    wscT_ = SC.f32(128)
    bmTM_ = SC.bf16(256)
    exal_ = SC.bf16(2, parts=64)
    rowd_ = SC.bf16(64, parts=1)
    gt_ = SC.f32(2 * LC)
    s16g_ = SC.bf16(512)
    hsTM_ = SC.bf16(1024)
    hsW_ = SC.bf16(1024)
    _prh_off = SC.top
    prh = [SC.f32(LC) for _ in range(2)]
    erh = [SC.f32(LC) for _ in range(2)]
    mth = [SC.bf16(LC) for _ in range(4)]
    ea_s = [SC.bf16(LC) for _ in range(2)]
    cmsc = [SC.bf16(LC) for _ in range(2)]
    sgroll = [mth[2], mth[3]]
    sqroll = [sc_t.ap().bitcast(BF16)[:, 2 * (_prh_off + i * LC):][:, 0:LC]
              for i in range(2)]
    rst_row_ap = SC.f32(LC, parts=1)
    rst16_ap = SC.bf16(LC, parts=1)

    class _A:
        def __init__(self, ap_):
            self._ap = ap_
        def ap(self):
            return self._ap
    acum, ea16, wsc, dtc, abt = _A(acum_), _A(ea16_), _A(wsc_), _A(dtc_), _A(abt_)
    negacumT, dtT, wscT, bmTM = _A(negacumT_), _A(dtT_), _A(wscT_), _A(bmTM_)
    exal, rowd, gt_sb, s16g = _A(exal_[:, 0:1]), _A(rowd_), _A(gt_), _A(s16g_)
    rst_row, rst16 = _A(rst_row_ap), _A(rst16_ap)
    hsTM_t, hsW_t = _A(hsTM_), _A(hsW_)
    bmcm_t, dtb_t = _A(bmcm_ap), _A(dtb_ap)

    def y_tile(ft):
        return yreg[:, ft * 256:(ft + 1) * 256]

    def sg_tile(i):
        return sgroll[i % 2]

    def scanprep(region, pair, xsrc, XK):
        tok0 = pair * 512
        for cti in range(NCT + 1):
            bi = cti % 2
            if cti % 2 == 0 and cti < NCT:
                wbuf = (cti // 2) % 2
                for cti2 in range(cti, min(cti + 2, NCT)):
                    dma("gpsimd",
                        wab[wbuf][:, (cti2 % 2) * NK * 128:][:, 0:NK * 128],
                        wa_t[cti2, :, :], writes=(f"wab{wbuf}",))
            if cti == NCT:
                dma("gpsimd", wdtb, wa_dt, writes=("wdtb",))
            for k in range(NK):
                if cti < NCT:
                    wbuf = (cti // 2) % 2
                    lhsTk = wab[wbuf][:, (cti % 2) * NK * 128 + k * 128:][:, 0:128]
                    out_ap = psf(bi, 512)
                    rkeys = (f"wab{wbuf}", XK[k])
                else:
                    lhsTk = wdtb[:, k * 64:(k + 1) * 64]
                    out_ap = psf(bi, 512, p=64)
                    rkeys = ("wdtb", XK[k])
                mm(out_ap, lhsTk, xsrc[k][:, tok0:tok0 + 512], k == 0, k == NK - 1,
                   reads=rkeys, writes=(f"ps{bi}",), inc=(k == NK - 1))
            pk = f"projc{bi}"
            if cti < NCT:
                act(proj_c[bi][:, 3:515], psf(bi, 512), AF.Identity,
                    bias=sb_ba.ap()[:, cti:cti + 1],
                    reads=(f"ps{bi}", "sb_ba"), writes=(pk,))
                cp(proj_c[bi][:, 0:3], prev3_t.ap()[:, cti * 3:cti * 3 + 3],
                   reads=("prev3",), writes=(pk,))
                cp(prev3_t.ap()[:, cti * 3:cti * 3 + 3], proj_c[bi][:, 512:515],
                   reads=(pk,), writes=("prev3",))
                ts(cacc_b[bi], proj_c[bi][:, 0:512], convw.ap()[:, cti * 4:cti * 4 + 1],
                   None, ALU.mult, reads=(pk, "convw"), writes=(f"cacc{bi}",))
                for kk in range(1, KCV):
                    stt(cacc_b[bi], proj_c[bi][:, kk:kk + 512],
                        convw.ap()[:, cti * 4 + kk:cti * 4 + kk + 1], cacc_b[bi],
                        ALU.mult, ALU.add, reads=(pk, "convw"), writes=(f"cacc{bi}",))
                for cc2 in range(2):
                    c = pair * 2 + cc2
                    if cti < 32:
                        dst = hs_slot(cti, c)
                        wk = f"hs{c}_{cti}"
                    else:
                        off = (0 if cti == 32 else 512) + cc2 * 256
                        dst = bmcm_t.ap()[:, off:off + 256]
                        wk = "bmcm"
                    act(dst, cacc_b[bi][:, cc2 * 256:(cc2 + 1) * 256], AF.Silu,
                        reads=(f"cacc{bi}",), writes=(wk,))
            else:
                act(proj_c[bi][0:64, 3:515], psf(bi, 512, p=64), AF.Identity,
                    bias=badt_comb.ap(),
                    reads=(f"ps{bi}", "badtc"), writes=(pk,))
                act(cacc_b[bi][0:64, 0:512], proj_c[bi][0:64, 3:515], AF.Abs,
                    reads=(pk,), writes=(f"cacc{bi}",))
                act(cacc_b[bi][0:64, 0:512], cacc_b[bi][0:64, 0:512], AF.Exp, scale=-1.0,
                    reads=(), writes=(f"cacc{bi}",))
                act(cacc_b[bi][0:64, 0:512], cacc_b[bi][0:64, 0:512], AF.Ln, bias=1.0,
                    reads=(), writes=(f"cacc{bi}",))
                ts(proj_c[bi][0:64, 3:515], proj_c[bi][0:64, 3:515], 0.0, None, ALU.max,
                   reads=(), writes=(pk,))
                tt(dtb_t.ap(), proj_c[bi][0:64, 3:515], cacc_b[bi][0:64, 0:512], ALU.add,
                   reads=(pk, f"cacc{bi}"), writes=("dtb",))

    def scancore(c, cc, out):
        if _KSUB < 1:
            return
        ts(abt.ap(), dtb_t.ap()[:, cc * 256:(cc + 1) * 256], aneg.ap(), None, ALU.mult,
           reads=("dtb", "aneg"), writes=("abt",))
        cp(dtc.ap(), dtb_t.ap()[:, cc * 256:(cc + 1) * 256],
           reads=("dtb",), writes=("dtc",))
        pr.op("vector", lambda e: e.tensor_tensor_scan(
            acum.ap(), onesr.ap(), abt.ap(), 0.0, op0=ALU.mult, op1=ALU.add),
            reads=("onesr", "abt"), writes=("acum",))
        act(ea16.ap(), acum.ap(), AF.Exp, reads=("acum",), writes=("ea16",))
        act(exal.ap(), acum.ap()[:, 255:256], AF.Exp, reads=("acum",), writes=("exal",))
        act(wsc.ap(), acum.ap(), AF.Exp, bias=acum.ap()[:, 255:256], scale=-1.0,
            reads=("acum",), writes=("wsc",))
        tt(wsc.ap(), wsc.ap(), dtc.ap(), ALU.mult, reads=("dtc",), writes=("wsc",))
        if _KSUB < 2:
            return
        for sb_i in range(2):
            for si, src in enumerate((acum, dtc, wsc)):
                tr(psf(1)[:, (si * 2 + sb_i) * 64:][:, 0:64],
                   src.ap()[:, sb_i * 128:(sb_i + 1) * 128], id64fs.ap(),
                   reads=(("acum", "dtc", "wsc")[si], "id64fs"), writes=("ps1",),
                   inc=(sb_i == 1 and si == 2))
        if _KSUB < 3:
            return
        ts(negacumT.ap(), psf(1)[:, 0:128], -1.0, None, ALU.mult,
           reads=("ps1",), writes=("negacumT",))
        cp(dtT.ap(), psf(1)[:, 128:256], reads=("ps1",), writes=("dtT",))
        cp(wscT.ap(), psf(1)[:, 256:384], reads=("ps1",), writes=("wscT",))
        for sb_i in range(2):
            tr(psh(1)[:, sb_i * 128:(sb_i + 1) * 128],
               bmcm_t.ap()[:, cc * 256 + sb_i * 128:][:, 0:128], id128b.ap(),
               reads=("bmcm", "id128b"), writes=("ps1",), inc=False)
        mm(psb[1].ap()[0:1, 256:320], exal.ap(), id64b.ap(), True, True,
           reads=("exal", "id64b"), writes=("ps1",))
        if _KSUB < 4:
            return
        cp(bmTM.ap(), psh(1)[:, 0:256], reads=("ps1",), writes=("bmTM",))
        cp(rowd.ap(), psb[1].ap()[0:1, 256:320], reads=("ps1",), writes=("rowd",))
        if out:
            for sb_i in range(2):
                mm(psf(1)[:, sb_i * 256:(sb_i + 1) * 256],
                   bmcm_t.ap()[:, cc * 256 + sb_i * 128:][:, 0:128],
                   bmcm_t.ap()[:, 512 + cc * 256:][:, 0:256], sb_i == 0, sb_i == 1,
                   reads=("bmcm",), writes=("ps1",), inc=(sb_i == 1))
            cp(gt_sb.ap(), psf(1), reads=("ps1",), writes=("gt",))

        if _KSUB < 5:
            return
        for g in range(8):
            for sb_i in range(2):
                for j in range(4):
                    ft = 4 * g + j
                    tr(psh(2 + sb_i)[:, j * 128:(j + 1) * 128],
                       hs_slot(ft, c)[:, sb_i * 128:(sb_i + 1) * 128],
                       id128b.ap(),
                       reads=(f"hs{c}_{ft}", "id128b"), writes=(f"ps{2 + sb_i}",),
                       inc=(j == 3))
            if _KSUB < 6:
                continue
            cp(hsTM_t.ap()[:, 0:512], psh(2)[:, 0:512], reads=("ps2",), writes=("hsTM",))
            cp(hsTM_t.ap()[:, 512:1024], psh(3)[:, 0:512], reads=("ps3",), writes=("hsTM",))
            for sb_i in range(2):
                for h8 in range(8):
                    h = g * 8 + h8
                    ts(hsW_t.ap()[:, sb_i * 512 + h8 * 64:][:, 0:64],
                       hsTM_t.ap()[:, sb_i * 512 + h8 * 64:][:, 0:64],
                       wscT.ap()[:, sb_i * 64 + h:][:, 0:1],
                       None, ALU.mult, reads=("hsTM", "wscT"), writes=("hsW",))
            if _KSUB < 7:
                continue
            cp(s16g.ap(), S[:, g * 512:(g + 1) * 512], reads=("S",), writes=("s16g",))

            if out:
                for h8 in range(8):
                    h = g * 8 + h8
                    pj = h // 2
                    ybank = 6 + (pj % 2)
                    bb = 2 + (h % 2)
                    eb = 4 + (h % 2)
                    mtb = [(h % 2) * 2, (h % 2) * 2 + 1]
                    hp = h % 2
                    for sb_i in range(2):
                        mm(psf(bb)[:, sb_i * 256:(sb_i + 1) * 256],
                           id64fs.ap()[:, h:h + 1].to_broadcast([64, 128]),
                           acum.ap(), sb_i == 0, False, reads=("id64fs", "acum"),
                           writes=(f"ps{bb}",), inc=False)
                        mm(psf(bb)[:, sb_i * 256:(sb_i + 1) * 256],
                           diagn_b.ap(),
                           lstr_b.ap()[:, sb_i * 256:(sb_i + 1) * 256],
                           False, sb_i == 1, reads=("diagn", "lstr"),
                           writes=(f"ps{bb}",), inc=(sb_i == 1))
                    mm(psf(eb)[:, 0:256], id64b.ap()[:, h:h + 1].to_broadcast([64, 128]),
                       ea16.ap(), True, True, reads=("id64b", "ea16"),
                       writes=(f"ps{eb}",))
                    for sb_i in range(2):
                        act(erh[sb_i], psf(bb)[:, sb_i * 256:(sb_i + 1) * 256], AF.Exp,
                            bias=negacumT.ap()[:, sb_i * 64 + h:][:, 0:1],
                            reads=(f"ps{bb}", "negacumT"), writes=(f"erh{sb_i}",))
                        stt(mth[mtb[sb_i]], erh[sb_i],
                            dtT.ap()[:, sb_i * 64 + h:][:, 0:1],
                            gt_sb.ap()[:, sb_i * 256:(sb_i + 1) * 256],
                            ALU.mult, ALU.mult,
                            reads=(f"erh{sb_i}", "dtT", "gt"),
                            writes=(f"mth{mtb[sb_i]}",))
                    act(ea_s[hp], psf(eb)[:, 0:256], AF.Identity,
                        reads=(f"ps{eb}",), writes=(f"ea_s{hp}",))
                    tt(cmsc[hp], bmcm_t.ap()[:, 512 + cc * 256:][:, 0:256], ea_s[hp],
                       ALU.mult, reads=("bmcm", f"ea_s{hp}"), writes=(f"cmsc{hp}",))
                    first = (h % 2 == 0)
                    for sb_i in range(2):
                        mm(psb[ybank].ap()[(h % 2) * 64:(h % 2) * 64 + 64, 0:256],
                           hsTM_t.ap()[:, sb_i * 512 + h8 * 64:][:, 0:64],
                           mth[mtb[sb_i]], sb_i == 0, False,
                           reads=("hsTM", f"mth{mtb[sb_i]}"), writes=(f"ps{ybank}",),
                           inc=False)
                    mm(psb[ybank].ap()[(h % 2) * 64:(h % 2) * 64 + 64, 0:256],
                       s16g.ap()[:, h8 * 64:(h8 + 1) * 64], cmsc[hp], False,
                       True, reads=("s16g", f"cmsc{hp}"), writes=(f"ps{ybank}",))
                    if h % 2 == 1:
                        stt(y_tile(pj), hs_slot(pj, c), sb_dexp.ap()[:, pj:pj + 1],
                            psf(ybank)[:, 0:256], ALU.mult, ALU.add,
                            reads=(f"hs{c}_{pj}", "sb_dexp", f"ps{ybank}"),
                            writes=(f"y{pj}",))
            if _KSUB < 8:
                continue
            for h8 in range(8):
                for sb_i in range(2):
                    mm(psf(0)[:, h8 * 64:(h8 + 1) * 64],
                       bmTM.ap()[:, sb_i * 128:(sb_i + 1) * 128],
                       hsW_t.ap()[:, sb_i * 512 + h8 * 64:][:, 0:64],
                       h8 == 0 and sb_i == 0, h8 == 7 and sb_i == 1,
                       reads=("bmTM", "hsW"), writes=("ps0",),
                       inc=(h8 == 7 and sb_i == 1))
            if _KSUB < 9:
                continue
            mm(psf(1), ones1b.ap(),
               rowd.ap()[0:1, g * 8:(g + 1) * 8].unsqueeze(2).to_broadcast([1, 8, 64]),
               True, True, reads=("ones1b", "rowd"), writes=("ps1",))
            tt(S[:, g * 512:(g + 1) * 512], S[:, g * 512:(g + 1) * 512], psf(1),
               ALU.mult, reads=("ps1", "s16g"), writes=("S",))
            tt(S[:, g * 512:(g + 1) * 512], S[:, g * 512:(g + 1) * 512], psf(0),
               ALU.add, reads=("ps0",), writes=("S",))

    def epilogue(c):
        for ft in range(NFT):
            if ft % 2 == 0:
                wbuf = (ft // 2) % 2
                for ft2 in (ft, ft + 1):
                    dma("gpsimd", wab[wbuf][:, (ft2 % 2) * NK * 128:][:, 0:NK * 128],
                        wb_t[ft2, :, :], writes=(f"wab{wbuf}",))
            bi = ft % 2
            wbuf = (ft // 2) % 2
            for k in range(NK):
                mm(psf(bi, 256), wab[wbuf][:, (ft % 2) * NK * 128 + k * 128:][:, 0:128],
                   xn[k][:, c * 256:(c + 1) * 256], k == 0, k == NK - 1,
                   reads=(f"wab{wbuf}", f"xn{k}"), writes=(f"ps{bi}",),
                   inc=(k == NK - 1))
            act(sg_tile(ft), psf(bi, 256), AF.Silu, bias=sb_bg.ap()[:, ft:ft + 1],
                reads=(f"ps{bi}", "sb_bg"), writes=(f"sg{ft % 2}",))
            tt(y_tile(ft), y_tile(ft), sg_tile(ft), ALU.mult,
               reads=(f"y{ft}", f"sg{ft % 2}"), writes=(f"y{ft}",))
            act(sqroll[ft % 2], y_tile(ft), AF.Square,
                reads=(f"y{ft}",), writes=(f"sq{ft % 2}",))
            mm(psb[4].ap()[0:1, 0:256], onescb.ap(), sqroll[ft % 2], ft == 0,
               ft == NFT - 1, reads=(f"sq{ft % 2}", "onescb"), writes=("ps4",),
               inc=(ft == NFT - 1))
        act(rst_row.ap(), psb[4].ap()[0:1, 0:256], AF.Sqrt, bias=epsc.ap()[0:1, 0:1],
            scale=1.0 / INTER, reads=("ps4", "epsc"), writes=("rstrow",))
        pr.op("vector", lambda e: e.reciprocal(rst_row.ap(), rst_row.ap()),
              reads=(), writes=("rstrow",))
        cp(rst16.ap(), rst_row.ap(), reads=("rstrow",), writes=("rst16",))
        mm(psf(5, 256), ones1b.ap(), rst16.ap(), True, True,
           reads=("ones1b", "rst16"), writes=("ps5",))
        for ft in range(NFT):
            stt(hs_slot(ft, c), y_tile(ft), sb_nw.ap()[:, ft:ft + 1], psf(5, 256),
                ALU.mult, ALU.mult, reads=(f"y{ft}", "sb_nw", "ps5"),
                writes=(f"hs{c}_{ft}",))

    def halo_mask():
        ts(S, S, sb_hm.ap(), None, ALU.mult, reads=("sb_hm",), writes=("S",))
        ts(prev3_t.ap(), prev3_t.ap(), sb_hm.ap(), None, ALU.mult,
           reads=("sb_hm",), writes=("prev3",))

    mset(S, 0.0, writes=("S",))
    if _TRUNC >= 4:
        for pairi in range(2):
            scanprep("h", pairi, xnh, XNH)
            if _TRUNC >= 5:
                for cc in range(2):
                    scancore(pairi * 2 + cc, cc, out=False)
        halo_mask()
    if _TRUNC >= 6:
        for pairi in range(2):
            scanprep("m", pairi, xn, XN)
            for cc in range(2):
                c = pairi * 2 + cc
                if _TRUNC >= 7:
                    scancore(c, cc, out=True)
                if _TRUNC >= 8:
                    epilogue(c)

    pr.fence()
    SC.pop()
    SC.pop()
    SC.push()
    wob = [SC.bf16(2 * NK * 128) for _ in range(2)]
    def ps_op(dt_i, c):
        return psb[dt_i // 2].ap()[:, (dt_i % 2) * 256:(dt_i % 2) * 256 + 256]
    for c in range(4 if _TRUNC >= 9 else 0):
        for ft in range(NFT):
            if ft % 2 == 0:
                wbuf = (ft // 2) % 2
                for ft2 in (ft, ft + 1):
                    dma("gpsimd", wob[wbuf][:, (ft2 % 2) * NK * 128:][:, 0:NK * 128],
                        wo_t[ft2, :, :], writes=(f"wob{wbuf}",))
            wbuf = (ft // 2) % 2
            for dt_i in range(16):
                mm(ps_op(dt_i, c),
                   wob[wbuf][:, (ft % 2) * NK * 128 + dt_i * 128:][:, 0:128],
                   hs_slot(ft, c), ft == 0 and dt_i % 2 == 0,
                   ft == NFT - 1 and dt_i % 2 == 1,
                   reads=(f"wob{wbuf}", f"hs{c}_{ft}"),
                   writes=(f"ps{dt_i // 2}",),
                   inc=(dt_i == 15))
        for dt_i in range(16):
            ts(acc_slot(dt_i, c), ps_op(dt_i, c), sb_bo.ap()[:, dt_i:dt_i + 1], None,
               ALU.add, reads=(f"ps{dt_i // 2}", "sb_bo"), writes=(f"acc{dt_i}",))

    pr.fence()
    SC.pop()
    SC.push()
    w13b = [SC.bf16(2 * NK * 128) for _ in range(4)]
    h2reg = SC.bf16(16 * TPC)
    w2g = [SC.bf16(2048) for _ in range(8)]
    be_ap = SC.bf16(TPC)
    ffn_s1 = SC.bf16(TPC)

    class _A2:
        def __init__(self, ap_):
            self._ap = ap_
        def ap(self):
            return self._ap
    be_s = _A2(be_ap)

    def h2_tile(i16):
        return h2reg[:, i16 * TPC:(i16 + 1) * TPC]

    for e_i in range(8 if _TRUNC >= 10 else 0):
        for h2_ in range(2):
            mm(psf(4 + h2_, 512),
               id64b.ap()[0:8, e_i:e_i + 1].to_broadcast([8, 128]),
               wfm.ap()[0:8, h2_ * 512:(h2_ + 1) * 512], True, True,
               reads=("id64b", "wfm"), writes=(f"ps{4 + h2_}",))
            cp(be_s.ap()[:, h2_ * 512:(h2_ + 1) * 512], psf(4 + h2_, 512),
               reads=(f"ps{4 + h2_}",), writes=("be",))
        for ih in range(2):
            for it16 in range(16):
                it = ih * 16 + it16
                if it16 % 2 == 0:
                    wb1 = (it16 // 2) % 2
                    for d2 in range(2):
                        dma("gpsimd",
                            w13b[wb1][:, d2 * NK * 128:][:, 0:NK * 128],
                            w1_t[e_i, it + d2, :, :], writes=(f"w13b{wb1}",))
                        dma("gpsimd",
                            w13b[2 + wb1][:, d2 * NK * 128:][:, 0:NK * 128],
                            w3_t[e_i, it + d2, :, :], writes=(f"w13b{2 + wb1}",))
                wb1 = (it16 // 2) % 2
                woff = (it16 % 2) * NK * 128
                pb1 = (it16 % 2) * 2
                pb3 = 4 + (it16 % 2) * 2
                for k in range(NK):
                    for hf in range(2):
                        mm(psf(pb1 + hf), w13b[wb1][:, woff + k * 128:][:, 0:128],
                           xn[k][:, hf * 512:(hf + 1) * 512], k == 0, k == NK - 1,
                           reads=(f"w13b{wb1}", f"xn{k}"), writes=(f"ps{pb1 + hf}",),
                           inc=(k == NK - 1))
                        mm(psf(pb3 + hf), w13b[2 + wb1][:, woff + k * 128:][:, 0:128],
                           xn[k][:, hf * 512:(hf + 1) * 512], k == 0, k == NK - 1,
                           reads=(f"w13b{2 + wb1}", f"xn{k}"), writes=(f"ps{pb3 + hf}",),
                           inc=(k == NK - 1))
                for hf in range(2):
                    act(ffn_s1[:, hf * 512:(hf + 1) * 512],
                        psf(pb1 + hf), AF.Silu,
                        reads=(f"ps{pb1 + hf}",), writes=("f_s1",))
                    tt(h2_tile(it16)[:, hf * 512:(hf + 1) * 512],
                       psf(pb3 + hf), be_s.ap()[:, hf * 512:(hf + 1) * 512], ALU.mult,
                       reads=(f"ps{pb3 + hf}", "be"), writes=(f"h2_{it16}",))
                tt(h2_tile(it16), h2_tile(it16), ffn_s1, ALU.mult,
                   reads=("f_s1",), writes=(f"h2_{it16}",))
            for itg in range(2):
                for j in range(8):
                    it = ih * 16 + itg * 8 + j
                    dma("gpsimd", w2g[j], w2_n[e_i, it * 128:(it + 1) * 128, :],
                        writes=(f"w2g{j}",))
                for dt_i in range(16):
                    pbk = dt_i % 2
                    for j in range(8):
                        it16 = itg * 8 + j
                        for hf in range(2):
                            mm(psf(pbk * 2 + hf),
                               w2g[j][:, dt_i * 128:(dt_i + 1) * 128],
                               h2_tile(it16)[:, hf * 512:(hf + 1) * 512],
                               j == 0, j == 7,
                               reads=(f"w2g{j}", f"h2_{it16}"),
                               writes=(f"ps{pbk * 2 + hf}",),
                               inc=(j == 7 and hf == 1))
                    for hf in range(2):
                        tt(arena[:, dt_i * 1024 + hf * 512:][:, 0:512],
                           arena[:, dt_i * 1024 + hf * 512:][:, 0:512],
                           psf(pbk * 2 + hf), ALU.add,
                           reads=(f"ps{pbk * 2 + hf}", f"acc{dt_i}"),
                           writes=(f"acc{dt_i}",))

    pr.fence()
    SC.pop()
    SC.push()
    xf = [SC.f32(D) for _ in range(2)]
    of = [SC.f32(D) for _ in range(2)]
    for tti in range(8):
        bi = tti % 2
        dma("sync", xf[bi], x_main[tti * 128:(tti + 1) * 128, :], writes=(f"xf{bi}",))
        for g4 in range(4):
            bank = 4 * (g4 % 2)
            for j in range(4):
                dt_i = g4 * 4 + j
                tr(psb[bank + j].ap()[:, 0:128],
                   arena[:, dt_i * 1024 + tti * 128:][:, 0:128],
                   id128fs.ap(),
                   reads=(f"acc{dt_i}", "id128fs"), writes=(f"ps{bank + j}",),
                   inc=(j == 3))
            for j in range(4):
                dt_i = g4 * 4 + j
                tt(of[bi][:, dt_i * 128:(dt_i + 1) * 128], xf[bi][:, dt_i * 128:(dt_i + 1) * 128],
                   psb[bank + j].ap()[:, 0:128], ALU.add,
                   reads=(f"ps{bank + j}", f"xf{bi}"), writes=(f"of{bi}",))
        dma("sync", o_out[tti * 128:(tti + 1) * 128, :], of[bi],
            reads=(f"of{bi}",), writes=(f"ost{bi}",))
    pr._wait("sync", pr.last_w.get("ost0"))
    pr._wait("sync", pr.last_w.get("ost1"))

    pr.replay()
    return nc


_NC_CACHE = {}

def _prep_static(inputs):
    f = np.float32
    W = np.ascontiguousarray
    ip_w = np.asarray(inputs["L0_attn_in_proj_w"], f)
    ip_b = np.asarray(inputs["L0_attn_in_proj_b"], f)
    wa = ip_w[:, INTER:INTER + CONV]
    wadt = ip_w[:, INTER + CONV:]
    wb = ip_w[:, :INTER]
    wo = np.asarray(inputs["L0_attn_out_proj_w"], f)
    w1 = np.asarray(inputs["L0_ffn_w1"], f)
    w3 = np.asarray(inputs["L0_ffn_w3"], f)
    w2 = np.asarray(inputs["L0_ffn_w2"], f)

    def tile_cols(w, nct):
        return W(w.reshape(NK, 128, nct, 128).transpose(2, 1, 0, 3)
                 .reshape(nct, 128, NK * 128))

    d = {}
    d["wa_t"] = tile_cols(wa, NCT)
    d["wa_dt"] = W(wadt.reshape(NK, 128, 64).transpose(1, 0, 2).reshape(128, NK * 64))
    d["wb_t"] = tile_cols(wb, NFT)
    d["wo_t"] = W(wo.reshape(NFT, 128, D))
    d["w1_t"] = W(np.stack([tile_cols(w1[e], NFT) for e in range(8)]))
    d["w3_t"] = W(np.stack([tile_cols(w3[e], NFT) for e in range(8)]))
    d["w2_n"] = W(w2)
    d["wg_t"] = W(np.asarray(inputs["L0_ffn_w_gate"], f).reshape(NK, 128, 8)
                  .transpose(1, 0, 2).reshape(128, NK * 8))
    cw = np.asarray(inputs["L0_attn_conv_w"], f)
    d["conv_w_t"] = W(cw.reshape(NCT, 128, KCV).transpose(1, 0, 2).reshape(128, NCT * KCV))
    d["ba_conv"] = W(ip_b[INTER:INTER + CONV].reshape(NCT, 128).T)
    d["ba_dtc"] = W(ip_b[INTER + CONV:].reshape(64, 1))
    d["bg_t"] = W(ip_b[:INTER].reshape(NFT, 128).T)
    d["bo_t"] = W(np.asarray(inputs["L0_attn_out_proj_b"], f).reshape(16, 128).T)
    d["dt_bias"] = W(np.asarray(inputs["L0_attn_dt_bias"], f).reshape(64, 1))
    d["a_log"] = W(np.asarray(inputs["L0_attn_A_log"], f).reshape(64, 1))
    d["d_exp"] = W(np.repeat(np.asarray(inputs["L0_attn_D"], f), 64)
                   .reshape(32, 128).T)
    d["nw_t"] = W(np.asarray(inputs["L0_attn_norm_weight"], f).reshape(32, 128).T)
    d["id128f"] = np.eye(128, dtype=f)
    d["id64f"] = np.eye(64, dtype=f)
    d["id64d_in"] = np.concatenate([np.eye(64, dtype=f)] * 2, axis=0)
    d["diagneg"] = np.eye(128, dtype=f) * -1.0e30
    ls = np.zeros((128, 2 * LC), f)
    for sb_i in range(2):
        srow = np.arange(128)[:, None] + sb_i * 128
        lcol = np.arange(LC)[None, :]
        ls[:, sb_i * LC:(sb_i + 1) * LC] = (lcol < srow).astype(f)
    d["lstrict"] = ls
    mk = np.zeros((128, 2 * LC), f)
    for sb_i in range(2):
        srow = np.arange(128)[:, None] + sb_i * 128
        lcol = np.arange(LC)[None, :]
        mk[:, sb_i * LC:(sb_i + 1) * LC] = np.where(lcol >= srow, 0.0, -1.0e30)
    d["maskc"] = mk
    return d


def kernel(**inputs):
    x = np.asarray(inputs["L0_attn_hidden_states"], np.float32)
    stat = _prep_static(inputs)
    nc = build_nc()
    in_maps = []
    for core in range(8):
        b, half = core // 2, core % 2
        m = dict(stat)
        m["x_main"] = np.ascontiguousarray(x[b, half * TPC:(half + 1) * TPC])
        m["x_halo"] = np.ascontiguousarray(x[b, 0:TPC])
        m["hmask"] = np.full((128, 1), float(half), np.float32)
        in_maps.append(m)
    res = run_bass_kernel_spmd(nc, in_maps, core_ids=list(range(8)))
    out = np.empty((4, 2048, D), np.float32)
    for core in range(8):
        b, half = core // 2, core % 2
        out[b, half * TPC:(half + 1) * TPC] = res.results[core]["o_out"]
    return out


if __name__ == "__main__":
    pass
